# version 06
# baseline (speedup 1.0000x reference)
import math
from contextlib import ExitStack

import numpy as np
import ml_dtypes

import concourse.bass as bass
import concourse.mybir as mybir
from concourse.bass_utils import run_bass_kernel_spmd

F32 = mybir.dt.float32
BF16 = mybir.dt.bfloat16
I32 = mybir.dt.int32
AF = mybir.ActivationFunctionType
ALU = mybir.AluOpType
AX = mybir.AxisListType

NCORE = 8
SEQ = 2
T = 2048
E = 1024
H, DH, DR, DKV = 16, 64, 32, 256
INNER = 4096
NST = 16
NTILE = SEQ * NST
SCALE = 1.0 / math.sqrt(DH + DR)
EPS = 1e-5
NEG = -1.0e30

ENGS = ("sync", "vector", "scalar", "tensor", "gpsimd")
DMAQ = ("sync", "scalar")
NDMASEM = 6

ALL_PHASES = ("p0", "p1", "p2", "p3preA", "p3preB", "p3a", "p3b")
P2C_PARTS = {"mm", "exp", "pool", "pv", "epi", "epi_all"}


class Prog:

    def __init__(self, nc):
        self.nc = nc
        self.sem_names = list(ENGS) + ["%s_dma%d" % (e, i) for e in DMAQ
                                       for i in range(NDMASEM)]
        self.cnt = {s: 0 for s in self.sem_names}
        self.q = {e: [] for e in ENGS}
        self.hw = {e: {s: 0 for s in self.sem_names} for e in ENGS}
        self.buf = {}
        self.dma_rr = {e: 0 for e in DMAQ}

    def op(self, eng, fn, reads=(), writes=(), dma=False):
        if dma:
            sem = "%s_dma%d" % (eng, self.dma_rr[eng])
            self.dma_rr[eng] = (self.dma_rr[eng] + 1) % NDMASEM
            inc = 16
        else:
            sem = eng
            inc = 1
        waits = {}
        if dma and self.cnt[sem] > 0:
            waits[sem] = self.cnt[sem]

        def dep(tok):
            if tok is None:
                return
            s, v = tok
            if s == "tensor" and eng == "tensor" and not dma:
                return
            if v > waits.get(s, 0):
                waits[s] = v

        for k in reads:
            st = self.buf.get(k)
            if st is not None:
                dep(st[0])
        for k in writes:
            st = self.buf.get(k)
            if st is not None:
                dep(st[0])
                for t in st[1]:
                    dep(t)
        tok = (sem, self.cnt[sem] + inc)
        self.cnt[sem] = tok[1]
        wl = []
        for s, v in waits.items():
            if v > self.hw[eng][s]:
                self.hw[eng][s] = v
                wl.append((s, v))
        self.q[eng].append({"waits": wl, "fn": fn, "sem": sem, "val": tok[1],
                            "dma": dma})
        for k in writes:
            self.buf[k] = [tok, []]
        for k in reads:
            st = self.buf.setdefault(k, [None, []])
            st[1].append(tok)
        return tok

    def barrier(self):
        snap = dict(self.cnt)
        for e in ENGS:
            wl = []
            for s, v in snap.items():
                if v > self.hw[e][s]:
                    self.hw[e][s] = v
                    wl.append((s, v))
            if wl:
                self.q[e].append({"waits": wl, "fn": None, "sem": e,
                                  "val": self.cnt[e], "dma": False})

    def build(self, ctx):
        sems = {s: ctx.enter_context(self.nc.semaphore("S_" + s))
                for s in self.sem_names}
        awaited = {s: set() for s in self.sem_names}
        for e in ENGS:
            for ent in self.q[e]:
                for s, v in ent["waits"]:
                    awaited[s].add(v)
        remap = {}
        for s in ENGS:
            remap[s] = {v: i + 1 for i, v in enumerate(sorted(awaited[s]))}

        blk = ctx.enter_context(self.nc.Block())

        def emit_engine(e):
            def body(eng):
                for ent in self.q[e]:
                    for s, v in ent["waits"]:
                        if s in remap:
                            eng.wait_ge(sems[s], remap[s][v])
                        else:
                            eng.wait_ge(sems[s], v)
                    if ent["fn"] is None:
                        continue
                    inst = ent["fn"](eng)
                    if ent["dma"]:
                        inst.then_inc(sems[ent["sem"]], 16)
                    elif ent["val"] in awaited[ent["sem"]]:
                        inst.then_inc(sems[ent["sem"]], 1)
            return body

        for e in ENGS:
            getattr(blk, e)(emit_engine(e))


def host_constants():
    c = {}
    c["c_ident"] = np.eye(128, dtype=ml_dtypes.bfloat16)
    r = np.arange(128)
    c["c_maskA"] = np.where(r[None, :] <= r[:, None], 0.0, NEG).astype(np.float32)
    c["c_maskT"] = np.where(r[:, None] <= r[None, :], 1.0, 0.0).astype(
        ml_dtypes.bfloat16)
    c["c_ones16"] = np.ones((128, 16), dtype=np.float32)
    c["c_onesrow"] = np.ones((1, 512), dtype=ml_dtypes.bfloat16)
    c["c_onesrowf"] = np.ones((1, 128), dtype=np.float32)
    c["c_zrow"] = np.zeros((1, 128), dtype=ml_dtypes.bfloat16)
    c["c_b1285"] = np.full((128, 1), 128.5, dtype=np.float32)
    c["c_b128t"] = np.full((128, 2048), 128.5, dtype=np.float32)
    c["c_zt"] = np.zeros((128, 1024), dtype=np.float32)
    c["c_identf"] = np.eye(128, dtype=np.float32)
    half = DR // 2
    inv = 1.0 / (10000.0 ** (np.arange(half, dtype=np.float64) / half))
    ang = np.arange(T, dtype=np.float64)[:, None] * inv[None, :]
    cos, sin = np.cos(ang), np.sin(ang)
    blk32_cos = np.concatenate([cos, cos], axis=1)
    blk32_sin = np.concatenate([sin, sin], axis=1)
    c["c_cosq"] = np.tile(blk32_cos * SCALE, (1, H)).astype(np.float32)
    c["c_sinq"] = np.tile(blk32_sin * SCALE, (1, H)).astype(np.float32)
    c["c_cosk"] = blk32_cos.astype(np.float32)
    c["c_sink"] = blk32_sin.astype(np.float32)
    return c


def build_program(dbg=(), phases=ALL_PHASES):
    nc = bass.Bass("TRN2", target_bir_lowering=False,
                   dynamic_dma_scratch_size=64)
    ctx = ExitStack()

    d = {}
    def din(name, shape, dtype):
        d[name] = nc.dram_tensor(name, shape, dtype, kind="ExternalInput").ap()
    din("x", [SEQ, T, E], F32)
    din("wq", [E, E], F32)
    din("wqr", [E, H * DR], F32)
    din("wkr", [E, DR], F32)
    din("wkvd", [E, DKV], F32)
    din("wku", [DKV, E], F32)
    din("wvu", [DKV, E], F32)
    din("wo", [E, E], F32)
    din("lobo", [1, H], F32)
    din("wgT", [E, INNER], F32)
    din("wuT", [E, INNER], F32)
    din("wdT", [INNER, E], F32)
    for name, arr_shape, dt in (
        ("c_ident", [128, 128], BF16), ("c_maskA", [128, 128], F32),
        ("c_maskT", [128, 128], BF16), ("c_ones16", [128, 16], F32),
        ("c_onesrow", [1, 512], BF16), ("c_onesrowf", [1, 128], F32),
        ("c_zrow", [1, 128], BF16), ("c_b1285", [128, 1], F32),
        ("c_b128t", [128, 2048], F32), ("c_zt", [128, 1024], F32),
        ("c_identf", [128, 128], F32),
        ("c_cosq", [T, H * DR], F32), ("c_sinq", [T, H * DR], F32),
        ("c_cosk", [T, DR], F32), ("c_sink", [T, DR], F32),
    ):
        din(name, arr_shape, dt)
    out_d = nc.dram_tensor("out", [SEQ, T, E], F32, kind="ExternalOutput").ap()
    attn_scr = nc.dram_tensor("attn_scr", [SEQ, NST, 128, E], F32,
                              kind="Internal").ap()
    x2_scr = nc.dram_tensor("x2_scr", [SEQ, NST, 128, E], F32,
                            kind="Internal").ap()
    xq_scr = nc.dram_tensor("xq_scr", [NTILE, 8, 128, 128], BF16,
                            kind="Internal").ap()
    gup_scr = nc.dram_tensor("gup_scr", [NTILE, 32, 128, 128], BF16,
                             kind="Internal").ap()

    p = Prog(nc)
    dbg_outs = {}

    def D(name, ap, reads):
        if name not in dbg:
            return
        o = nc.dram_tensor("dbg_" + name, list(ap.shape), ap.dtype,
                           kind="ExternalOutput").ap()
        dbg_outs[name] = o
        p.op("sync", lambda e, o=o, ap=ap: e.dma_start(out=o, in_=ap),
             reads=reads, dma=True)

    def sb(name, shape, dtype, scope=ctx):
        return scope.enter_context(nc.sbuf_tensor(name, shape, dtype))

    def sb_end(t_handle, name):
        m = nc.lookup_mloc(name)
        return m.addr + m.size() // 128

    def pad_to(scope, target, tag):
        probe = scope.enter_context(nc.sbuf_tensor("pp_" + tag, [128, 32],
                                                   mybir.dt.uint8))
        cur = nc.lookup_mloc("pp_" + tag).addr
        need = target - (cur + 32)
        assert need >= 0, (tag, cur, target)
        if need > 0:
            scope.enter_context(nc.sbuf_tensor("pd_" + tag, [128, need],
                                               mybir.dt.uint8))

    def pst(name, shape, dtype, scope=ctx):
        return scope.enter_context(nc.psum_tensor(name, shape, dtype))

    c_ident = sb("sb_c_ident", [128, 128], BF16)
    c_maskA = sb("sb_c_maskA", [128, 128], F32)
    c_maskT = sb("sb_c_maskT", [128, 128], BF16)
    c_ones16 = sb("sb_c_ones16", [128, 16], F32)
    c_onesrow = sb("sb_c_onesrow", [1, 512], BF16)
    c_onesrowf = sb("sb_c_onesrowf", [1, 128], F32)
    c_zrow = sb("sb_c_zrow", [1, 128], BF16)
    c_b1285 = sb("sb_c_b1285", [128, 1], F32)
    c_identf = sb("sb_c_identf", [128, 128], F32)
    lb_row = sb("lb_row", [1, 32], F32)
    x_st = sb("x_st", [128, E], F32)
    scr = sb("scr", [128, E], F32)
    st = sb("st", [128, 16], F32)
    st2 = sb("st2", [128, 16], F32)
    Cbc = sb("Cbc", [128, 256], F32)
    C256 = sb("C256", [1, 256], BF16)
    a127 = sb("a127", [128, NTILE], F32)
    fg_all = sb("fg_all", [128, NTILE], F32)
    fu_all = sb("fu_all", [128, NTILE], F32)
    fd_all = sb("fd_all", [128, NTILE], F32)
    swbc = {}
    mwbc = {}
    for w in ("g", "u", "dn"):
        swbc[w] = sb("swbc_" + w, [128, 1], F32)
        mwbc[w] = sb("mwbc_" + w, [128, 1], F32)

    psP = pst("psP", [128, 1024], F32)
    psS = pst("psS", [128, 512], F32)
    psT = pst("psT", [128, 1024], BF16)

    def dma(queue, out_ap, in_ap, reads=(), writes=()):
        return p.op(queue, lambda e, o=out_ap, i=in_ap: e.dma_start(out=o, in_=i),
                    reads=reads, writes=writes, dma=True)

    def ln_stats(src_ap, src_key):
        p.op("scalar", lambda e: e.activation(out=scr[:], in_=src_ap,
                                              func=AF.Square,
                                              accum_out=st[:, 1:2]),
             reads=[src_key],
             writes=[("scr",), ("scr2",), ("scr3",), ("st", 1)])
        p.op("vector", lambda e: e.reduce_sum(out=st[:, 0:1], in_=src_ap,
                                              axis=AX.X),
             reads=[src_key], writes=[("st", 0)])
        p.op("vector", lambda e: e.tensor_scalar_mul(st[:, 2:3], st[:, 0:1],
                                                     1.0 / E),
             reads=[("st", 0)], writes=[("st", 2)])
        p.op("vector", lambda e: e.tensor_scalar_mul(st[:, 3:4], st[:, 1:2],
                                                     1.0 / E),
             reads=[("st", 1)], writes=[("st", 3)])
        p.op("vector", lambda e: e.tensor_tensor(st[:, 4:5], st[:, 2:3],
                                                 st[:, 2:3], ALU.mult),
             reads=[("st", 2)], writes=[("st", 4)])
        p.op("vector", lambda e: e.scalar_tensor_tensor(
            out=st[:, 5:6], in0=st[:, 3:4], scalar=EPS, in1=st[:, 4:5],
            op0=ALU.add, op1=ALU.subtract),
             reads=[("st", 3), ("st", 4)], writes=[("st", 5)])
        p.op("scalar", lambda e: e.activation(out=st[:, 6:7], in_=st[:, 5:6],
                                              func=AF.Sqrt),
             reads=[("st", 5)], writes=[("st", 6)])
        p.op("vector", lambda e: e.reciprocal(st[:, 7:8], st[:, 6:7]),
             reads=[("st", 6)], writes=[("st", 7)])
        p.op("vector", lambda e: e.tensor_tensor(st[:, 8:9], st[:, 2:3],
                                                 st[:, 7:8], ALU.mult),
             reads=[("st", 2), ("st", 7)], writes=[("st", 8)])
        p.op("vector", lambda e: e.tensor_scalar_mul(st[:, 9:10], st[:, 8:9],
                                                     -1.0),
             reads=[("st", 8)], writes=[("st", 9)])

    def ln_apply(src_ap, src_key, dst_ap, dst_key):
        p.op("vector", lambda e: e.tensor_scalar(
            out=dst_ap, in0=src_ap, scalar1=st[:, 7:8], scalar2=st[:, 9:10],
            op0=ALU.mult, op1=ALU.add),
            reads=[src_key, ("st", 7), ("st", 9)], writes=[dst_key])

    def transpose8(src_sb, src_key, dst_sb, dst_key, n=8):
        for f in range(n):
            p.op("tensor", lambda e, f=f: e.transpose(
                out=psT[:, 128 * f:128 * f + 128],
                in_=src_sb[:, 128 * f:128 * f + 128], identity=c_ident[:]),
                reads=[src_key, ("ident",)], writes=[("psT",)])
        p.op("vector", lambda e: e.tensor_copy(out=dst_sb[:, 0:128 * n],
                                               in_=psT[:, 0:128 * n]),
             reads=[("psT",)], writes=[dst_key])

    def load_weight_bf16(dst_sb, dst_key, src_d, kdim, ndim, stg_sb, stg_key,
                         stage_cols):
        total = kdim * ndim
        done = 0
        while done < total:
            cols = min(stage_cols, total - done)
            k0, n_off = divmod(done, ndim)
            assert n_off == 0 and cols % ndim == 0
            kk = cols // ndim
            src = src_d.rearrange("(k p) n -> p k n", p=128)[:, k0:k0 + kk, :]
            dma("sync", stg_sb[:, 0:cols].rearrange("p (k n) -> p k n", k=kk),
                src, writes=[stg_key])
            p.op("vector",
                 lambda e, a=done, c=cols: e.tensor_copy(
                     out=dst_sb[:, a:a + c], in_=stg_sb[:, 0:c]),
                 reads=[stg_key], writes=[dst_key])
            done += cols

    if "p0" in phases:
        for name, sbuf in (("c_ident", c_ident), ("c_maskA", c_maskA),
                           ("c_maskT", c_maskT), ("c_ones16", c_ones16),
                           ("c_onesrow", c_onesrow), ("c_onesrowf", c_onesrowf),
                           ("c_zrow", c_zrow), ("c_b1285", c_b1285),
                           ("c_identf", c_identf)):
            dma("sync", sbuf[:], d[name], writes=[(name.replace("c_", ""),)])
        dma("sync", lb_row[0:1, 16:32], d["lobo"], writes=[("lb",)])
        p.op("scalar", lambda e: e.activation(out=lb_row[0:1, 0:16],
                                              in_=lb_row[0:1, 16:32],
                                              func=AF.Exp),
             reads=[("lb",)], writes=[("lbe",)])
        for h in range(H):
            p.op("vector", lambda e, h=h: e.tensor_scalar_mul(
                C256[0:1, 16 * h:16 * h + 16], c_onesrow[0:1, 0:16],
                lb_row[0:1, h:h + 1]),
                reads=[("lbe",), ("onesrow",)], writes=[("C256", h)])
        p.op("tensor", lambda e: e.matmul(out=psS[:, 0:256],
                                          lhsT=c_onesrow[0:1, 0:128],
                                          rhs=C256[0:1, 0:256],
                                          start=True, stop=True),
             reads=[("onesrow",)] + [("C256", h) for h in range(H)],
             writes=[("psS",)])
        p.op("vector", lambda e: e.tensor_copy(out=Cbc[:], in_=psS[:, 0:256]),
             reads=[("psS",)], writes=[("Cbc",)])

    att_scope = ExitStack()
    QS = sb("QS", [128, 8 * 4096], BF16, att_scope)
    KS = sb("KS", [128, 8 * 4096], BF16, att_scope)
    v65 = sb("v65", [128, NST * 1040], BF16, att_scope)
    h_bf = sb("h_bf", [128, E], BF16, att_scope)
    hfm = sb("hfm", [128, E], BF16, att_scope)
    pj_bf = sb("pj_bf", [128, E], BF16, att_scope)
    qr_bf = sb("qr_bf", [128, 512], BF16, att_scope)
    latfm = sb("latfm", [128, 256], BF16, att_scope)
    cosk_sb = sb("cosk_sb", [128, NST * DR], F32, att_scope)
    sink_sb = sb("sink_sb", [128, NST * DR], F32, att_scope)
    m_all = sb("m_all", [128, 256], F32, att_scope)
    Em_all = sb("Em_all", [128, 256], F32, att_scope)
    probsT = sb("probsT", [128, 1024], BF16, att_scope)
    mdiag = sb("mdiag", [128, 128], BF16, att_scope)
    CEm_all = sb("CEm_all", [128, 256], F32, att_scope)
    appiece = sb("appiece", [128, 512], F32, att_scope)

    def stack_ap(stacks, h, rows, col0, width):
        base = 2048 * h
        rr = {"all": (0, 128), "qc": (0, 64), "ro": (64, 96)}[rows]
        return stacks[rr[0]:rr[1], base + col0: base + col0 + width]

    if "p1" in phases:
        for stacks, key in ((QS, ("QS",)), (KS, ("KS",))):
            p.op("vector", lambda e, stacks=stacks: e.memset(
                stacks[64:128, :], 0.0), writes=[key])

    def emit_p1a(s):
        sc = ExitStack()
        wkvd = sb("wkvd_%d" % s, [128, 8 * DKV], BF16, sc)
        wku = sb("wku_%d" % s, [128, 2 * E], BF16, sc)
        wvu = sb("wvu_%d" % s, [128, 2 * E], BF16, sc)
        wkr = sb("wkr_%d" % s, [128, 8 * DR], BF16, sc)
        stg = sb("stg_%d" % s, [128, 2048], F32, sc)
        load_weight_bf16(wkvd, ("wkvd",), d["wkvd"], 8, DKV, stg, ("stg",), 2048)
        load_weight_bf16(wku, ("wku",), d["wku"], 2, E, stg, ("stg",), 2048)
        load_weight_bf16(wvu, ("wvu",), d["wvu"], 2, E, stg, ("stg",), 2048)
        load_weight_bf16(wkr, ("wkr",), d["wkr"], 8, DR, stg, ("stg",), 256)
        if s == 0:
            dma("sync", cosk_sb[:].rearrange("p (s j) -> p s j", s=NST),
                d["c_cosk"].rearrange("(s p) j -> p s j", p=128),
                writes=[("cosk",)])
            dma("sync", sink_sb[:].rearrange("p (s j) -> p s j", s=NST),
                d["c_sink"].rearrange("(s p) j -> p s j", p=128),
                writes=[("sink",)])
        for i in range(NST):
            dma("sync", x_st[:], d["x"][s, 128 * i:128 * i + 128, :],
                reads=[("x_st",)], writes=[("x_st",)])
            ln_stats(x_st[:], ("x_st",))
            ln_apply(x_st[:], ("x_st",), h_bf[:], ("h_bf",))
            transpose8(h_bf, ("h_bf",), hfm, ("hfm",))
            for k in range(8):
                p.op("tensor", lambda e, k=k: e.matmul(
                    out=psP[:, 0:256], lhsT=hfm[:, 128 * k:128 * k + 128],
                    rhs=wkvd[:, DKV * k:DKV * (k + 1)],
                    start=(k == 0), stop=(k == 7)),
                    reads=[("hfm",), ("wkvd",)], writes=[("psP",)])
            p.op("vector", lambda e: e.tensor_copy(out=pj_bf[:, 0:256],
                                                   in_=psP[:, 0:256]),
                 reads=[("psP",)], writes=[("pj",)])
            for f in range(2):
                p.op("tensor", lambda e, f=f: e.transpose(
                    out=psT[:, 128 * f:128 * f + 128],
                    in_=pj_bf[:, 128 * f:128 * f + 128], identity=c_ident[:]),
                    reads=[("pj",), ("ident",)], writes=[("psT",)])
            p.op("vector", lambda e: e.tensor_copy(out=latfm[:],
                                                   in_=psT[:, 0:256]),
                 reads=[("psT",)], writes=[("latfm",)])
            for c in range(2):
                for n in range(2):
                    p.op("tensor", lambda e, c=c, n=n: e.matmul(
                        out=psP[:, 512 * n:512 * (n + 1)],
                        lhsT=latfm[:, 128 * c:128 * (c + 1)],
                        rhs=wku[:, E * c + 512 * n:E * c + 512 * (n + 1)],
                        start=(c == 0), stop=(c == 1)),
                        reads=[("latfm",), ("wku",)], writes=[("psP",)])
            p.op("vector", lambda e: e.tensor_copy(out=pj_bf[:], in_=psP[:]),
                 reads=[("psP",)], writes=[("pj",)])
            for r in range(2):
                for hh in range(8):
                    h = 8 * r + hh
                    p.op("tensor", lambda e, h=h, hh=hh: e.transpose(
                        out=psT[0:64, 128 * hh:128 * hh + 128],
                        in_=pj_bf[:, 64 * h:64 * h + 64], identity=c_ident[:]),
                        reads=[("pj",), ("ident",)], writes=[("psT",)])
                p.op("vector", lambda e, i=i, r=r: e.tensor_copy(
                    out=KS[:].rearrange("p (h c) -> p h c", h=16)[
                        0:64, 8 * r:8 * r + 8, 128 * i:128 * i + 128],
                    in_=psT[0:64, :].rearrange("p (g c) -> p g c", g=8)),
                    reads=[("psT",)], writes=[("KS",)])
            for c in range(2):
                for n in range(2):
                    p.op("tensor", lambda e, c=c, n=n: e.matmul(
                        out=psP[:, 512 * n:512 * (n + 1)],
                        lhsT=latfm[:, 128 * c:128 * (c + 1)],
                        rhs=wvu[:, E * c + 512 * n:E * c + 512 * (n + 1)],
                        start=(c == 0), stop=(c == 1)),
                        reads=[("latfm",), ("wvu",)], writes=[("psP",)])
            v_i = v65[:, 1040 * i:1040 * (i + 1)].rearrange(
                "p (h c) -> p h c", h=H)
            p.op("vector", lambda e, v_i=v_i: e.tensor_copy(
                out=v_i[:, :, 0:64],
                in_=psP[:].rearrange("p (h c) -> p h c", h=H)),
                reads=[("psP",)], writes=[("v65",)])
            p.op("vector", lambda e, v_i=v_i: e.tensor_copy(
                out=v_i[:, :, 64:65],
                in_=c_ones16[:].rearrange("p (h c) -> p h c", h=H)),
                reads=[("ones16",)], writes=[("v65",)])
            for k in range(8):
                p.op("tensor", lambda e, k=k: e.matmul(
                    out=psS[:, 0:DR], lhsT=hfm[:, 128 * k:128 * k + 128],
                    rhs=wkr[:, DR * k:DR * (k + 1)],
                    start=(k == 0), stop=(k == 7)),
                    reads=[("hfm",), ("wkr",)], writes=[("psS",)])
            ck = cosk_sb[:, DR * i:DR * (i + 1)]
            sk = sink_sb[:, DR * i:DR * (i + 1)]
            p.op("vector", lambda e, ck=ck: e.tensor_tensor(
                scr[:, 0:DR], psS[:, 0:DR], ck, ALU.mult),
                reads=[("psS",), ("cosk",)], writes=[("scr",)])
            p.op("vector", lambda e, sk=sk: e.tensor_tensor(
                scr[:, 512:528], psS[:, 16:32], sk[:, 16:32], ALU.mult),
                reads=[("psS",), ("sink",)], writes=[("scr2",)])
            p.op("vector", lambda e, sk=sk: e.tensor_tensor(
                scr[:, 768:784], psS[:, 0:16], sk[:, 0:16], ALU.mult),
                reads=[("psS",), ("sink",)], writes=[("scr3",)])
            p.op("vector", lambda e: e.tensor_tensor(
                qr_bf[:, 0:16], scr[:, 0:16], scr[:, 512:528], ALU.subtract),
                reads=[("scr",), ("scr2",)], writes=[("qr",)])
            p.op("vector", lambda e: e.tensor_tensor(
                qr_bf[:, 16:32], scr[:, 16:32], scr[:, 768:784], ALU.add),
                reads=[("scr",), ("scr3",)], writes=[("qr",)])
            p.op("tensor", lambda e: e.transpose(
                out=psT[0:32, 0:128], in_=qr_bf[:, 0:DR],
                identity=c_ident[:]),
                reads=[("qr",), ("ident",)], writes=[("psT",)])
            for h in range(H):
                ap_dst = stack_ap(KS, h, "ro", 128 * i, 128)
                p.op("vector", lambda e, ap_dst=ap_dst: e.tensor_copy(
                    out=ap_dst, in_=psT[0:32, 0:128]),
                    reads=[("psT",)], writes=[("KS",)])
        sc.close()
        p.barrier()

    def emit_p1b(s):
        sc = ExitStack()
        wqsb = sb("wq_%d" % s, [128, 8 * E], BF16, sc)
        stg = sb("stgq_%d" % s, [128, 2048], F32, sc)
        load_weight_bf16(wqsb, ("wq",), d["wq"], 8, E, stg, ("stg",), 2048)
        for i in range(NST):
            dma("sync", x_st[:], d["x"][s, 128 * i:128 * i + 128, :],
                reads=[("x_st",)], writes=[("x_st",)])
            ln_stats(x_st[:], ("x_st",))
            ln_apply(x_st[:], ("x_st",), h_bf[:], ("h_bf",))
            D("x_%d_%d" % (s, i), x_st[:], [("x_st",)])
            D("h_%d_%d" % (s, i), h_bf[:], [("h_bf",)])
            transpose8(h_bf, ("h_bf",), hfm, ("hfm",))
            D("hfm_%d_%d" % (s, i), hfm[:], [("hfm",)])
            for k in range(8):
                for n in range(2):
                    p.op("tensor", lambda e, k=k, n=n: e.matmul(
                        out=psP[:, 512 * n:512 * (n + 1)],
                        lhsT=hfm[:, 128 * k:128 * k + 128],
                        rhs=wqsb[:, E * k + 512 * n:E * k + 512 * (n + 1)],
                        start=(k == 0), stop=(k == 7)),
                        reads=[("hfm",), ("wq",)], writes=[("psP",)])
            p.op("vector", lambda e: e.tensor_scalar_mul(pj_bf[:], psP[:],
                                                         SCALE),
                 reads=[("psP",)], writes=[("pj",)])
            D("qc_%d_%d" % (s, i), pj_bf[:], [("pj",)])
            D("wqdump_%d_%d" % (s, i), wqsb[:], [("wq",)])
            for r in range(2):
                for hh in range(8):
                    h = 8 * r + hh
                    p.op("tensor", lambda e, h=h, hh=hh: e.transpose(
                        out=psT[0:64, 128 * hh:128 * hh + 128],
                        in_=pj_bf[:, 64 * h:64 * h + 64], identity=c_ident[:]),
                        reads=[("pj",), ("ident",)], writes=[("psT",)])
                p.op("vector", lambda e, i=i, r=r: e.tensor_copy(
                    out=QS[:].rearrange("p (h c) -> p h c", h=16)[
                        0:64, 8 * r:8 * r + 8, 128 * i:128 * i + 128],
                    in_=psT[0:64, :].rearrange("p (g c) -> p g c", g=8)),
                    reads=[("psT",)], writes=[("QS",)])
        sc.close()
        p.barrier()
        sc = ExitStack()
        wqrsb = sb("wqr_%d" % s, [128, 8 * 512], BF16, sc)
        stg = sb("stgr_%d" % s, [128, 2048], F32, sc)
        cq = sb("cq_%d" % s, [128, 512], F32, sc)
        sq = sb("sq_%d" % s, [128, 512], F32, sc)
        load_weight_bf16(wqrsb, ("wqr",), d["wqr"], 8, 512, stg, ("stg",), 2048)
        for i in range(NST):
            dma("sync", x_st[:], d["x"][s, 128 * i:128 * i + 128, :],
                reads=[("x_st",)], writes=[("x_st",)])
            dma("sync", cq[:], d["c_cosq"][128 * i:128 * i + 128, :],
                reads=[("cq",)], writes=[("cq",)])
            dma("sync", sq[:], d["c_sinq"][128 * i:128 * i + 128, :],
                reads=[("sq",)], writes=[("sq",)])
            ln_stats(x_st[:], ("x_st",))
            ln_apply(x_st[:], ("x_st",), h_bf[:], ("h_bf",))
            transpose8(h_bf, ("h_bf",), hfm, ("hfm",))
            for k in range(8):
                p.op("tensor", lambda e, k=k: e.matmul(
                    out=psS[:, 0:512], lhsT=hfm[:, 128 * k:128 * k + 128],
                    rhs=wqrsb[:, 512 * k:512 * (k + 1)],
                    start=(k == 0), stop=(k == 7)),
                    reads=[("hfm",), ("wqr",)], writes=[("psS",)])
            ps3 = psS[:, 0:512].rearrange("p (h t j) -> p h t j", h=H, t=2)
            cq3 = cq[:].rearrange("p (h t j) -> p h t j", h=H, t=2)
            sq3 = sq[:].rearrange("p (h t j) -> p h t j", h=H, t=2)
            t13 = scr[:, 0:512].rearrange("p (h t j) -> p h t j", h=H, t=2)
            m13 = scr[:, 512:768].rearrange("p (h j) -> p h j", h=H)
            m23 = scr[:, 768:1024].rearrange("p (h j) -> p h j", h=H)
            qb3 = qr_bf[:].rearrange("p (h t j) -> p h t j", h=H, t=2)
            p.op("vector", lambda e: e.tensor_tensor(
                scr[:, 0:512], psS[:, 0:512], cq[:], ALU.mult),
                reads=[("psS",), ("cq",)], writes=[("scr",)])
            p.op("vector", lambda e: e.tensor_tensor(
                m13, ps3[:, :, 1, :], sq3[:, :, 1, :], ALU.mult),
                reads=[("psS",), ("sq",)], writes=[("scr2",)])
            p.op("vector", lambda e: e.tensor_tensor(
                m23, ps3[:, :, 0, :], sq3[:, :, 0, :], ALU.mult),
                reads=[("psS",), ("sq",)], writes=[("scr3",)])
            p.op("vector", lambda e: e.tensor_tensor(
                qb3[:, :, 0, :], t13[:, :, 0, :], m13, ALU.subtract),
                reads=[("scr",), ("scr2",)], writes=[("qr",)])
            p.op("vector", lambda e: e.tensor_tensor(
                qb3[:, :, 1, :], t13[:, :, 1, :], m23, ALU.add),
                reads=[("scr",), ("scr3",)], writes=[("qr",)])
            for r in range(2):
                for hh in range(8):
                    h = 8 * r + hh
                    p.op("tensor", lambda e, h=h, hh=hh: e.transpose(
                        out=psT[64:96, 128 * hh:128 * hh + 128],
                        in_=qr_bf[:, 32 * h:32 * h + 32], identity=c_ident[:]),
                        reads=[("qr",), ("ident",)], writes=[("psT",)])
                p.op("vector", lambda e, i=i, r=r: e.tensor_copy(
                    out=QS[:].rearrange("p (h c) -> p h c", h=16)[
                        64:96, 8 * r:8 * r + 8, 128 * i:128 * i + 128],
                    in_=psT[64:96, :].rearrange("p (g c) -> p g c", g=8)),
                    reads=[("psT",)], writes=[("QS",)])
        sc.close()
        p.barrier()

    def emit_p2a(s):
        sc = ExitStack()
        psA = pst("psA_%d" % s, [128, 2048], F32, sc)
        for h in range(H):
            for i in range(NST):
                ki = (i + 1) * 128
                nch = (ki + 511) // 512
                for c in range(nch):
                    w = min(512, ki - 512 * c)
                    p.op("tensor", lambda e, h=h, i=i, c=c, w=w: e.matmul(
                        out=psA[:, 512 * c:512 * c + w],
                        lhsT=stack_ap(QS, h, "all", 128 * i, 128),
                        rhs=stack_ap(KS, h, "all", 512 * c, w),
                        start=True, stop=True),
                        reads=[("QS",), ("KS",)], writes=[("psA",)])
                p.op("vector", lambda e, i=i: e.tensor_tensor(
                    scr[:, 0:128], psA[:, 128 * i:128 * i + 128], c_maskA[:],
                    ALU.add),
                    reads=[("psA",), ("maskA",)], writes=[("scr",)])
                col = 16 * h + i
                if i > 0:
                    p.op("vector", lambda e, i=i: e.reduce_max(
                        out=st2[:, 0:1], in_=psA[:, 0:128 * i], axis=AX.X),
                        reads=[("psA",)], writes=[("st2", 0)])
                    p.op("vector", lambda e: e.reduce_max(
                        out=st2[:, 1:2], in_=scr[:, 0:128], axis=AX.X),
                        reads=[("scr",)], writes=[("st2", 1)])
                    p.op("vector", lambda e, col=col: e.tensor_tensor(
                        m_all[:, col:col + 1], st2[:, 0:1], st2[:, 1:2],
                        ALU.max),
                        reads=[("st2", 0), ("st2", 1)], writes=[("m_all",)])
                else:
                    p.op("vector", lambda e, col=col: e.reduce_max(
                        out=m_all[:, col:col + 1], in_=scr[:, 0:128],
                        axis=AX.X),
                        reads=[("scr",)], writes=[("m_all",)])
        sc.close()
        p.barrier()
        p.op("scalar", lambda e: e.activation(out=Em_all[:], in_=m_all[:],
                                              func=AF.Exp),
             reads=[("m_all",)], writes=[("Em",)])
        p.op("vector", lambda e: e.tensor_tensor(CEm_all[:], Em_all[:],
                                                 Cbc[:], ALU.mult),
             reads=[("Em",), ("Cbc",)], writes=[("CEm",)])

    def emit_p2c(s):
        sc = ExitStack()
        psB = pst("psB_%d" % s, [128, 1024], F32, sc)
        psPV = pst("psPV_%d" % s, [128, 1024], F32, sc)
        for h in range(H):
            for HF in range(2):
                q_base = 1024 * HF
                for n in range(2):
                    p.op("tensor", lambda e, n=n: e.matmul(
                        out=psPV[:, 512 * n:512 * (n + 1)],
                        lhsT=c_zrow[0:1, :], rhs=c_onesrow[0:1, 0:512],
                        start=True, stop=False),
                        reads=[("zrow",), ("onesrow",)], writes=[("psPV",)])
                last_in_bank = {}
                for j in range(8 * HF + 8):
                    for sl in range(8):
                        if 8 * HF + sl >= j:
                            last_in_bank[sl // 4] = (j, sl)
                for j in range(8 * HF + 8):
                    q0 = max(q_base, 128 * j)
                    a = q0 - q_base
                    for seg0, seg1 in ((0, 512), (512, 1024)):
                        lo = max(a, seg0)
                        if lo >= seg1 or "mm" not in P2C_PARTS:
                            continue
                        p.op("tensor",
                             lambda e, h=h, j=j, lo=lo, seg1=seg1,
                             q_base=q_base: e.matmul(
                                 out=psB[:, lo:seg1],
                                 lhsT=stack_ap(KS, h, "all", 128 * j, 128),
                                 rhs=stack_ap(QS, h, "all", q_base + lo,
                                              seg1 - lo),
                                 start=True, stop=True),
                             reads=[("KS",), ("QS",)], writes=[("psB",)])
                    if "exp" in P2C_PARTS:
                        p.op("scalar", lambda e, a=a: e.activation(
                            out=probsT[:, a:1024], in_=psB[:, a:1024],
                            func=AF.Exp),
                            reads=[("psB",)], writes=[("probsT",)])
                    if 128 * j >= q_base and "pool" in P2C_PARTS:
                        p.op("gpsimd", lambda e, a=a: e.tensor_tensor(
                            mdiag[:], probsT[:, a:a + 128],
                            c_maskT[:], ALU.mult),
                            reads=[("probsT",), ("maskT",)],
                            writes=[("mdiag",)])
                    D("pb_%d_%d_%d_%d" % (s, h, HF, j), probsT[:],
                      [("probsT",)])
                    for sl in range(8):
                        sg = 8 * HF + sl
                        if sg < j:
                            continue
                        stop_f = (last_in_bank[sl // 4] == (j, sl))
                        if "pv" not in P2C_PARTS:
                            continue
                        if sg == j:
                            lhs_ap = mdiag[:]
                            lhs_key = ("mdiag",)
                        else:
                            lhs_ap = probsT[:, 128 * sg - q_base:
                                            128 * sg - q_base + 128]
                            lhs_key = ("probsT",)
                        p.op("tensor",
                             lambda e, h=h, j=j, sl=sl, lhs_ap=lhs_ap,
                             stop_f=stop_f: e.matmul(
                                 out=psPV[:, 128 * sl:128 * sl + 65],
                                 lhsT=lhs_ap,
                                 rhs=v65[:, 1040 * j + 65 * h:
                                         1040 * j + 65 * h + 65],
                                 start=False, stop=stop_f),
                             reads=[lhs_key, ("v65",)],
                             writes=[("psPV",)])
                if "epi" not in P2C_PARTS:
                    continue
                pv3 = psPV[:].rearrange("p (s c) -> p s c", s=8)
                if "epi_stt" in P2C_PARTS or "epi_all" in P2C_PARTS:
                    p.op("vector", lambda e, h=h, HF=HF, pv3=pv3:
                         e.scalar_tensor_tensor(
                             out=st2[:, 0:8], in0=pv3[:, :, 64:65].rearrange(
                                 "p s c -> p (s c)"),
                             scalar=1.0,
                             in1=CEm_all[:, 16 * h + 8 * HF:
                                         16 * h + 8 * HF + 8],
                             op0=ALU.mult, op1=ALU.add),
                         reads=[("psPV",), ("CEm",)], writes=[("st2", 0)])
                if "epi_recip" in P2C_PARTS or "epi_all" in P2C_PARTS:
                    p.op("vector", lambda e: e.reciprocal(st2[:, 8:16],
                                                          st2[:, 0:8]),
                         reads=[("st2", 0)], writes=[("st2", 8)])
                if "epi_ts" in P2C_PARTS or "epi_all" in P2C_PARTS:
                    for sl in range(8):
                        p.op("vector", lambda e, sl=sl: e.tensor_scalar(
                            out=appiece[:, 64 * sl:64 * sl + 64],
                            in0=psPV[:, 128 * sl:128 * sl + 64],
                            scalar1=st2[:, 8 + sl:9 + sl], scalar2=None,
                            op0=ALU.mult),
                            reads=[("psPV",), ("st2", 8)], writes=[("app",)])
                if "epi_dma" in P2C_PARTS or "epi_all" in P2C_PARTS:
                    dst = attn_scr[s, 8 * HF:8 * HF + 8, :,
                                   64 * h:64 * h + 64]
                    dst = dst.rearrange("s p c -> p s c")
                    src = appiece[:].rearrange("p (s c) -> p s c", s=8)
                    dma("scalar", dst, src, reads=[("app",)],
                        writes=[("attn_scr", s, 8 * HF + k)
                                for k in range(8)])
        sc.close()
        p.barrier()

    if "p1" in phases or "p2" in phases:
        for s in range(SEQ):
            if "p1" in phases:
                emit_p1a(s)
                emit_p1b(s)
                D("QS%d" % s, QS[:], [("QS",)])
                D("KS%d" % s, KS[:], [("KS",)])
                D("v65_%d" % s, v65[:], [("v65",)])
            if "p2" in phases or "p2a_only" in phases:
                emit_p2a(s)
                D("m%d" % s, m_all[:], [("m_all",)])
                D("CEm%d" % s, CEm_all[:], [("CEm",)])
                if "p2" in phases:
                    emit_p2c(s)
    att_scope.close()
    p.barrier()
    if "p2" in phases:
        D("attn", attn_scr,
          [("attn_scr", s, i) for s in range(SEQ) for i in range(NST)])

    if "p3preA" in phases:
        sc = ExitStack()
        wosb = sb("wo_sb", [128, 8 * E], BF16, sc)
        stg = sb("stg_o", [128, 2048], F32, sc)
        ap_ld = sb("ap_ld", [128, E], F32, sc)
        apb = sb("apb", [128, E], BF16, sc)
        afm = sb("afm", [128, E], BF16, sc)
        x2_sb = sb("x2_sb", [128, E], F32, sc)
        h2_sb = sb("h2_sb", [128, E], F32, sc)
        u_sb = sb("u_sb", [128, E], F32, sc)
        i_sb = sb("i_sb", [128, E], I32, sc)
        lt_sb = sb("lt_sb", [128, E], F32, sc)
        xq_sb = sb("xq_sb", [128, E], BF16, sc)
        xqfm = sb("xqfm", [128, E], BF16, sc)
        a_cb128 = sb("cb128_a", [128, E], F32, sc)
        a_czt = sb("czt_a", [128, 1024], F32, sc)
        dma("sync", a_cb128[:], d["c_b128t"][:, 0:E], writes=[("cb128",)])
        dma("sync", a_czt[:], d["c_zt"], writes=[("czt",)])
        load_weight_bf16(wosb, ("wo",), d["wo"], 8, E, stg, ("stg",), 2048)
        for s in range(SEQ):
            for i in range(NST):
                t = 16 * s + i
                dma("sync", ap_ld[:], attn_scr[s, i],
                    reads=[("ap_ld",), ("attn_scr", s, i)], writes=[("ap_ld",)])
                p.op("vector", lambda e: e.tensor_copy(out=apb[:],
                                                       in_=ap_ld[:]),
                     reads=[("ap_ld",)], writes=[("apb",)])
                transpose8(apb, ("apb",), afm, ("afm",))
                for k in range(8):
                    for n in range(2):
                        p.op("tensor", lambda e, k=k, n=n: e.matmul(
                            out=psP[:, 512 * n:512 * (n + 1)],
                            lhsT=afm[:, 128 * k:128 * k + 128],
                            rhs=wosb[:, E * k + 512 * n:E * k + 512 * (n + 1)],
                            start=(k == 0), stop=(k == 7)),
                            reads=[("afm",), ("wo",)], writes=[("psP",)])
                dma("sync", x_st[:], d["x"][s, 128 * i:128 * i + 128, :],
                    reads=[("x_st",)], writes=[("x_st",)])
                p.op("vector", lambda e: e.tensor_tensor(
                    x2_sb[:], psP[:], x_st[:], ALU.add),
                    reads=[("psP",), ("x_st",)], writes=[("x2",)])
                dma("scalar", x2_scr[s, i], x2_sb[:], reads=[("x2",)],
                    writes=[("x2_scr", s, i)])
                ln_stats(x2_sb[:], ("x2",))
                ln_apply(x2_sb[:], ("x2",), h2_sb[:], ("h2",))
                p.op("vector", lambda e: e.tensor_reduce(
                    out=st[:, 10:11], in_=h2_sb[:], axis=AX.X, op=ALU.max,
                    apply_absolute_value=True),
                    reads=[("h2",)], writes=[("st", 10)])
                p.op("vector", lambda e: e.tensor_scalar_max(
                    st[:, 11:12], st[:, 10:11], EPS),
                    reads=[("st", 10)], writes=[("st", 11)])
                p.op("vector", lambda e: e.reciprocal(st[:, 12:13],
                                                      st[:, 11:12]),
                     reads=[("st", 11)], writes=[("st", 12)])
                p.op("vector", lambda e: e.tensor_scalar_mul(
                    st[:, 13:14], st[:, 12:13], 127.0),
                    reads=[("st", 12)], writes=[("st", 13)])
                p.op("vector", lambda e, t=t: e.tensor_scalar_mul(
                    a127[:, t:t + 1], st[:, 11:12], 1.0 / 127.0),
                    reads=[("st", 11)], writes=[("a127", t)])
                p.op("vector", lambda e: e.scalar_tensor_tensor(
                    out=u_sb[:], in0=h2_sb[:], scalar=st[:, 13:14],
                    in1=a_cb128[:, 0:E], op0=ALU.mult, op1=ALU.add),
                    reads=[("h2",), ("st", 13), ("cb128",)], writes=[("u",)])
                p.op("vector", lambda e: e.tensor_tensor(
                    i_sb[:], u_sb[:], a_czt[:], ALU.add),
                    reads=[("u",), ("czt",)], writes=[("i32",)])
                p.op("vector", lambda e: e.tensor_tensor(
                    lt_sb[:], u_sb[:], i_sb[:], ALU.is_lt),
                    reads=[("u",), ("i32",)], writes=[("lt",)])
                p.op("vector", lambda e: e.scalar_tensor_tensor(
                    out=xq_sb[:], in0=i_sb[:], scalar=128.0, in1=lt_sb[:],
                    op0=ALU.subtract, op1=ALU.subtract),
                    reads=[("i32",), ("lt",)], writes=[("xq",)])
                transpose8(xq_sb, ("xq",), xqfm, ("xqfm",))
                dma("scalar",
                    xq_scr[t].rearrange("c p n -> p c n"),
                    xqfm[:].rearrange("p (c n) -> p c n", c=8),
                    reads=[("xqfm",)], writes=[("xq_scr", t)])
        sc.close()
        p.barrier()
        D("x2", x2_scr,
          [("x2_scr", s, i) for s in range(SEQ) for i in range(NST)])
        D("xq", xq_scr, [("xq_scr", t) for t in range(NTILE)])

    def quant_weight_pieces(dst_sb, dst_key, src_d, n_pieces, piece_cols,
                            wname, stg, u2, i2, lt2, cb128, czt,
                            dst_off=0):
        src3 = src_d.rearrange("(k p) n -> p k n", p=128)
        ndim = src_d.shape[1]
        per_k = ndim
        segs = dst_sb if isinstance(dst_sb, list) else [
            (dst_sb, dst_key, dst_sb.shape[1])]
        for pc in range(n_pieces):
            c0 = pc * piece_cols
            k0, n_off = divmod(c0, per_k)
            assert n_off % piece_cols == 0 or per_k % piece_cols == 0
            if per_k >= piece_cols:
                src = src3[:, k0:k0 + 1, n_off:n_off + piece_cols]
            else:
                kk = piece_cols // per_k
                src = src3[:, k0:k0 + kk, :]
            dma("sync", stg[:, 0:piece_cols].rearrange(
                "p (k n) -> p k n", n=min(per_k, piece_cols)), src,
                reads=[("stg3",)], writes=[("stg3",)])
            p.op("vector", lambda e: e.tensor_scalar(
                out=u2[:, 0:piece_cols], in0=stg[:, 0:piece_cols],
                scalar1=swbc[wname][:, 0:1], scalar2=None, op0=ALU.mult),
                reads=[("stg3",), ("swmw", wname, "s")],
                writes=[("u2",)])
            p.op("vector", lambda e: e.tensor_scalar(
                out=lt2[:, 0:piece_cols], in0=u2[:, 0:piece_cols],
                scalar1=0.5, scalar2=None, op0=ALU.is_gt),
                reads=[("u2",)], writes=[("lt2",)])
            p.op("vector", lambda e: e.tensor_scalar(
                out=i2[:, 0:piece_cols].bitcast(F32),
                in0=u2[:, 0:piece_cols],
                scalar1=-0.5, scalar2=None, op0=ALU.is_lt),
                reads=[("u2",)], writes=[("i2",)])
            cc = c0
            for seg_sb, seg_key, seg_n in segs:
                if cc < seg_n:
                    break
                cc -= seg_n
            assert cc + piece_cols <= seg_n
            p.op("vector", lambda e, seg_sb=seg_sb, cc=cc: e.tensor_tensor(
                seg_sb[:, cc:cc + piece_cols], lt2[:, 0:piece_cols],
                i2[:, 0:piece_cols].bitcast(F32), ALU.subtract),
                reads=[("lt2",), ("i2",)], writes=[seg_key])

    if "p3preB" in phases:
        sc = ExitStack()
        stg = sb("stg_s", [128, 4096], F32, sc)
        accb = sb("accb", [128, 64], F32, sc)
        prow = sb("prow", [128, 16], F32, sc)
        for wname, src_d, nch in (("g", d["wgT"], 8), ("u", d["wuT"], 8),
                                  ("dn", d["wdT"], 8)):
            for c in range(nch):
                if wname == "dn":
                    srcap = src_d.rearrange("(k p) n -> p k n", p=128)[
                        :, 4 * c:4 * c + 4, :]
                    dmain = stg[:].rearrange("p (k n) -> p k n", k=4)
                else:
                    srcap = src_d.rearrange("(k p) n -> p k n", p=128)[
                        :, c:c + 1, :]
                    dmain = stg[:].rearrange("p (k n) -> p k n", k=1)
                dma("sync", dmain, srcap, reads=[("stg4",)],
                    writes=[("stg4",)])
                p.op("vector", lambda e, c=c: e.tensor_reduce(
                    out=accb[:, 8 * c:8 * c + 8],
                    in_=stg[:].rearrange("p (g n) -> p g n", g=8),
                    axis=AX.X, op=ALU.add, apply_absolute_value=True),
                    reads=[("stg4",)], writes=[("accb", c)])
            p.op("vector", lambda e: e.tensor_reduce(
                out=prow[:, 0:1], in_=accb[:], axis=AX.X, op=ALU.add),
                reads=[("accb", c) for c in range(8)], writes=[("prow",)])
            p.op("tensor", lambda e: e.transpose(
                out=psS[0:1, 0:128], in_=prow[:, 0:1],
                identity=c_identf[:]),
                reads=[("prow",), ("identf",)], writes=[("psS",)])
            p.op("vector", lambda e: e.tensor_reduce(
                out=prow[0:1, 0:8], in_=psS[0:1, 0:128].rearrange(
                    "p (g n) -> p g n", g=8), axis=AX.X, op=ALU.add),
                reads=[("psS",)], writes=[("prow2",)])
            p.op("vector", lambda e: e.tensor_reduce(
                out=st2[0:1, 0:1], in_=prow[0:1, 0:8], axis=AX.X, op=ALU.add),
                reads=[("prow2",)], writes=[("st2", 0)])
            p.op("vector", lambda e: e.tensor_scalar_mul(
                st2[0:1, 2:3], st2[0:1, 0:1], 1.0 / (INNER * E)),
                reads=[("st2", 0)], writes=[("st2", 2)])
            p.op("vector", lambda e: e.tensor_scalar_max(
                st2[0:1, 3:4], st2[0:1, 2:3], EPS),
                reads=[("st2", 2)], writes=[("st2", 3)])
            p.op("vector", lambda e: e.reciprocal(st2[0:1, 4:5],
                                                  st2[0:1, 3:4]),
                 reads=[("st2", 3)], writes=[("st2", 4)])
            for srccol, dst, role in ((3, mwbc[wname], "m"),
                                      (4, swbc[wname], "s")):
                p.op("tensor", lambda e, srccol=srccol: e.matmul(
                    out=psS[:, 4:5], lhsT=c_onesrowf[0:1, :],
                    rhs=st2[0:1, srccol:srccol + 1], start=True, stop=True),
                    reads=[("st2", srccol), ("onesrowf",)], writes=[("psS",)])
                p.op("vector", lambda e, dst=dst: e.tensor_copy(
                    out=dst[:], in_=psS[:, 4:5]),
                    reads=[("psS",)], writes=[("swmw", wname, role)])
        p.op("vector", lambda e: e.tensor_scalar(
            out=fg_all[:], in0=a127[:], scalar1=mwbc["g"][:, 0:1],
            scalar2=None, op0=ALU.mult),
            reads=[("swmw", "g", "m")] + [("a127", t) for t in
                                               range(NTILE)],
            writes=[("fg",)])
        p.op("vector", lambda e: e.tensor_scalar(
            out=fu_all[:], in0=a127[:], scalar1=mwbc["u"][:, 0:1],
            scalar2=None, op0=ALU.mult),
            reads=[("swmw", "u", "m")] + [("a127", t) for t in
                                               range(NTILE)],
            writes=[("fu",)])
        D("a127", a127[:], [("a127", t) for t in range(NTILE)])
        D("fg", fg_all[:], [("fg",)])
        D("fu", fu_all[:], [("fu",)])
        sc.close()
        p.barrier()

    if "p3a" in phases:
        sc = ExitStack()
        u2 = sb("u2", [128, 1024], F32, sc)
        i2 = sb("i2", [128, 1024], I32, sc)
        lt2 = sb("lt2", [128, 1024], F32, sc)
        cb128 = sb("cb128_b", [128, 1024], F32, sc)
        czt = sb("czt_b", [128, 1024], F32, sc)
        u2_addr = nc.lookup_mloc("u2").addr
        assert u2_addr < 65536
        wqgL = sb("wqgL", [128, 3 * INNER], BF16, sc)
        pad_to(sc, 63456, "a0")
        xqfm2 = sb("xqfm2", [128, E], BF16, sc)
        pad_to(sc, 65536, "a0b")
        stg3 = sb("stg3", [128, 2048], F32, sc)
        gates = sb("gates", [128, 1024], F32, sc)
        pad_to(sc, u2_addr + 65536, "a1")
        sb("sacr_a", [128, 12288], mybir.dt.uint8, sc)
        gup = sb("gup", [128, INNER], F32, sc)
        gupq = sb("gupq", [128, INNER], BF16, sc)
        gqf = sb("gqf", [128, 1024], BF16, sc)
        wqgM = sb("wqgM", [128, INNER], BF16, sc)
        pad_to(sc, 131040, "a2")
        wqgB = sb("wqgB", [128, 4 * INNER], BF16, sc)
        wqu = sb("wqu", [128, 8 * INNER], BF16, sc)
        assert sb_end(wqu, "wqu") <= 229344, sb_end(wqu, "wqu")
        assert nc.lookup_mloc("xqfm2").addr == 63456
        for nm in ("gup", "gupq", "gqf", "gates", "stg3"):
            mm_ = nc.lookup_mloc(nm)
            assert mm_.addr >= 65536 and \
                mm_.addr + mm_.size() // 128 <= 131072, nm
        psU = pst("psU", [128, 1024], F32, sc)
        psG = pst("psG", [128, 1024], F32, sc)
        dma("sync", cb128[:], d["c_b128t"][:, 0:1024], writes=[("cb128",)])
        dma("sync", czt[:], d["c_zt"], writes=[("czt",)])
        wqg_segs = [(wqgL, ("wqgL",), 3 * INNER), (wqgM, ("wqgM",), INNER),
                    (wqgB, ("wqgB",), 4 * INNER)]
        quant_weight_pieces(wqg_segs, None, d["wgT"], 32, 1024, "g",
                            stg3, u2, i2, lt2, cb128, czt)
        quant_weight_pieces(wqu, ("wqu",), d["wuT"], 32, 1024, "u",
                            stg3, u2, i2, lt2, cb128, czt)

        def wqg_ap(k, lo, hi):
            if k < 3:
                return wqgL[:, INNER * k + lo:INNER * k + hi], ("wqgL",)
            if k < 4:
                return wqgM[:, lo:hi], ("wqgM",)
            return wqgB[:, INNER * (k - 4) + lo:INNER * (k - 4) + hi], ("wqgB",)

        for t in range(NTILE):
            dma("sync", xqfm2[:].rearrange("p (c n) -> p c n", c=8),
                xq_scr[t].rearrange("c p n -> p c n"),
                reads=[("xqfm2",), ("xq_scr", t)], writes=[("xqfm2",)])
            for Q in range(4):
                for k in range(8):
                    for n in range(2):
                        gap, gkey = wqg_ap(k, 1024 * Q + 512 * n,
                                           1024 * Q + 512 * (n + 1))
                        p.op("tensor", lambda e, k=k, n=n, gap=gap: e.matmul(
                            out=psG[:, 512 * n:512 * (n + 1)],
                            lhsT=xqfm2[:, 128 * k:128 * k + 128],
                            rhs=gap, start=(k == 0), stop=(k == 7)),
                            reads=[("xqfm2",), gkey], writes=[("psG",)])
                for k in range(8):
                    for n in range(2):
                        p.op("tensor", lambda e, k=k, n=n, Q=Q: e.matmul(
                            out=psU[:, 512 * n:512 * (n + 1)],
                            lhsT=xqfm2[:, 128 * k:128 * k + 128],
                            rhs=wqu[:, INNER * k + 1024 * Q + 512 * n:
                                    INNER * k + 1024 * Q + 512 * (n + 1)],
                            start=(k == 0), stop=(k == 7)),
                            reads=[("xqfm2",), ("wqu",)], writes=[("psU",)])
                p.op("scalar", lambda e, t=t: e.activation(
                    out=gates[:], in_=psG[:], func=AF.Silu,
                    scale=fg_all[:, t:t + 1]),
                    reads=[("psG",), ("fg",)], writes=[("gates",)])
                p.op("vector", lambda e, t=t, Q=Q: e.scalar_tensor_tensor(
                    out=gup[:, 1024 * Q:1024 * (Q + 1)], in0=psU[:],
                    scalar=fu_all[:, t:t + 1], in1=gates[:],
                    op0=ALU.mult, op1=ALU.mult),
                    reads=[("psU",), ("fu",), ("gates",)],
                    writes=[("gup", Q)])
                p.op("vector", lambda e, Q=Q: e.tensor_reduce(
                    out=st2[:, Q:Q + 1],
                    in_=gup[:, 1024 * Q:1024 * (Q + 1)], axis=AX.X,
                    op=ALU.max, apply_absolute_value=True),
                    reads=[("gup", Q)], writes=[("st2", Q)])
            p.op("vector", lambda e: e.tensor_reduce(
                out=st2[:, 4:5], in_=st2[:, 0:4], axis=AX.X, op=ALU.max),
                reads=[("st2", 0), ("st2", 1), ("st2", 2), ("st2", 3)],
                writes=[("st2", 4)])
            p.op("vector", lambda e: e.tensor_scalar_max(
                st2[:, 5:6], st2[:, 4:5], EPS),
                reads=[("st2", 4)], writes=[("st2", 5)])
            p.op("vector", lambda e: e.reciprocal(st2[:, 6:7], st2[:, 5:6]),
                 reads=[("st2", 5)], writes=[("st2", 6)])
            p.op("vector", lambda e: e.tensor_scalar_mul(
                st2[:, 7:8], st2[:, 6:7], 127.0),
                reads=[("st2", 6)], writes=[("st2", 7)])
            p.op("vector", lambda e: e.tensor_scalar_mul(
                st2[:, 9:10], st2[:, 5:6], 1.0 / 127.0),
                reads=[("st2", 5)], writes=[("st2", 9)])
            p.op("vector", lambda e, t=t: e.tensor_scalar(
                out=fd_all[:, t:t + 1], in0=st2[:, 9:10],
                scalar1=mwbc["dn"][:, 0:1], scalar2=None, op0=ALU.mult),
                reads=[("st2", 9), ("swmw", "dn", "m")],
                writes=[("fd", t)])
            for Q in range(4):
                p.op("vector", lambda e, Q=Q: e.scalar_tensor_tensor(
                    out=u2[:, 0:1024], in0=gup[:, 1024 * Q:1024 * (Q + 1)],
                    scalar=st2[:, 7:8], in1=cb128[:, 0:1024],
                    op0=ALU.mult, op1=ALU.add),
                    reads=[("gup", Q), ("st2", 7), ("cb128",)],
                    writes=[("u2",)])
                p.op("vector", lambda e: e.tensor_tensor(
                    i2[:, 0:1024], u2[:, 0:1024], czt[:, 0:1024], ALU.add),
                    reads=[("u2",), ("czt",)], writes=[("i2",)])
                p.op("vector", lambda e: e.tensor_tensor(
                    lt2[:, 0:1024], u2[:, 0:1024], i2[:, 0:1024], ALU.is_lt),
                    reads=[("u2",), ("i2",)], writes=[("lt2",)])
                p.op("vector", lambda e, Q=Q: e.scalar_tensor_tensor(
                    out=gupq[:, 1024 * Q:1024 * (Q + 1)], in0=i2[:, 0:1024],
                    scalar=128.0, in1=lt2[:, 0:1024],
                    op0=ALU.subtract, op1=ALU.subtract),
                    reads=[("i2",), ("lt2",)], writes=[("gupq", Q)])
            D("wq_t%d" % t, wqu[:], [("wqu",)])
            D("gup_%d" % t, gup[:], [("gup", 0), ("gup", 1), ("gup", 2),
                                     ("gup", 3)])
            D("gupq_%d" % t, gupq[:], [("gupq", 0), ("gupq", 1),
                                       ("gupq", 2), ("gupq", 3)])
            for r in range(4):
                for f in range(8):
                    p.op("tensor", lambda e, r=r, f=f: e.transpose(
                        out=psT[:, 128 * f:128 * f + 128],
                        in_=gupq[:, 1024 * r + 128 * f:
                                 1024 * r + 128 * f + 128],
                        identity=c_ident[:]),
                        reads=[("gupq", r), ("ident",)], writes=[("psT",)])
                p.op("vector", lambda e: e.tensor_copy(out=gqf[:],
                                                       in_=psT[:]),
                     reads=[("psT",)], writes=[("gqf",)])
                dma("scalar",
                    gup_scr[t, 8 * r:8 * r + 8].rearrange("c p n -> p c n"),
                    gqf[:].rearrange("p (c n) -> p c n", c=8),
                    reads=[("gqf",)], writes=[("gup_scr", t, r)])
        sc.close()
        p.barrier()

    if "p3b" in phases:
        sc = ExitStack()
        b_u2 = sb("u2b", [128, 1024], F32, sc)
        b_i2 = sb("i2b", [128, 1024], I32, sc)
        b_lt2 = sb("lt2b", [128, 1024], F32, sc)
        b_cb128 = sb("cb128_c", [128, 1024], F32, sc)
        b_czt = sb("czt_c", [128, 1024], F32, sc)
        u2_addr = nc.lookup_mloc("u2b").addr
        assert u2_addr < 65536
        wqdL = sb("wqdL", [128, 9 * E], BF16, sc)
        gqfm = sb("gqfm", [128, INNER], BF16, sc)
        x2b = sb("x2b", [128, E], F32, sc)
        pad_to(sc, 65536, "b0b")
        b_stg3 = sb("stg3b", [128, 2048], F32, sc)
        pad_to(sc, u2_addr + 65536, "b1")
        sb("sacr_b", [128, 12288], mybir.dt.uint8, sc)
        out_sb = sb("out_sb", [128, E], F32, sc)
        wqdM = sb("wqdM", [128, 11 * E], BF16, sc)
        pad_to(sc, 131040, "b2")
        wqdB = sb("wqdB", [128, 12 * E], BF16, sc)
        assert nc.lookup_mloc("gqfm").addr == 53216
        for nm in ("out_sb", "stg3b", "wqdM"):
            mm_ = nc.lookup_mloc(nm)
            assert mm_.addr >= 65536 and \
                mm_.addr + mm_.size() // 128 <= 65536 + 53216, nm
        dma("sync", b_cb128[:], d["c_b128t"][:, 0:1024], writes=[("cb128",)])
        dma("sync", b_czt[:], d["c_zt"], writes=[("czt",)])
        wqd_segs = [(wqdL, ("wqdL",), 9 * E), (wqdM, ("wqdM",), 11 * E),
                    (wqdB, ("wqdB",), 12 * E)]
        quant_weight_pieces(wqd_segs, None, d["wdT"], 32, 1024, "dn",
                            b_stg3, b_u2, b_i2, b_lt2, b_cb128, b_czt)

        def wqd_ap(k, lo, hi):
            if k < 9:
                return wqdL[:, E * k + lo:E * k + hi], ("wqdL",)
            if k < 20:
                return wqdM[:, E * (k - 9) + lo:E * (k - 9) + hi], ("wqdM",)
            return wqdB[:, E * (k - 20) + lo:E * (k - 20) + hi], ("wqdB",)

        for t in range(NTILE):
            s, i = divmod(t, NST)
            dma("sync", gqfm[:].rearrange("p (c n) -> p c n", c=32),
                gup_scr[t].rearrange("c p n -> p c n"),
                reads=[("gqfm",)] + [("gup_scr", t, r) for r in range(4)],
                writes=[("gqfm",)])
            dma("sync", x2b[:], x2_scr[s, i],
                reads=[("x2b",), ("x2_scr", s, i)], writes=[("x2b",)])
            if t == 8:
                D("x2b8", x2b[:], [("x2b",)])
                D("gqfm8", gqfm[:], [("gqfm",)])
            for k in range(32):
                for n in range(2):
                    dap, dkey = wqd_ap(k, 512 * n, 512 * (n + 1))
                    p.op("tensor", lambda e, k=k, n=n, dap=dap: e.matmul(
                        out=psP[:, 512 * n:512 * (n + 1)],
                        lhsT=gqfm[:, 128 * k:128 * k + 128],
                        rhs=dap, start=(k == 0), stop=(k == 31)),
                        reads=[("gqfm",), dkey], writes=[("psP",)])
            p.op("vector", lambda e, t=t: e.scalar_tensor_tensor(
                out=out_sb[:], in0=psP[:], scalar=fd_all[:, t:t + 1],
                in1=x2b[:], op0=ALU.mult, op1=ALU.add),
                reads=[("psP",), ("fd", t), ("x2b",)], writes=[("out_sb",)])
            if t == 8:
                D("out_sb8", out_sb[:], [("out_sb",)])
                D("fd8", fd_all[:], [("fd", tt) for tt in range(NTILE)])
            dma("scalar", out_d[s, 128 * i:128 * i + 128, :], out_sb[:],
                reads=[("out_sb",)], writes=[("out_d", t)])
        sc.close()

    p.barrier()
    p.build(ctx)
    ctx.close()
    return nc, dbg_outs


def make_inputs(full_inputs, core):
    f32 = lambda a: np.ascontiguousarray(np.asarray(a, dtype=np.float32))
    x = f32(full_inputs["L0_attn_x"])
    m = {
        "x": np.ascontiguousarray(x[SEQ * core:SEQ * (core + 1)]),
        "wq": f32(full_inputs["L0_attn_Wq"]),
        "wqr": f32(full_inputs["L0_attn_Wqr"]),
        "wkr": f32(full_inputs["L0_attn_Wkr"]),
        "wkvd": f32(full_inputs["L0_attn_Wkvd"]),
        "wku": f32(full_inputs["L0_attn_Wku"]),
        "wvu": f32(full_inputs["L0_attn_Wvu"]),
        "wo": f32(full_inputs["L0_attn_Wo"]),
        "lobo": f32(full_inputs["L0_attn_lobo_log"]).reshape(1, H),
        "wgT": np.ascontiguousarray(f32(full_inputs["L0_ffn_w_gate"]).T),
        "wuT": np.ascontiguousarray(f32(full_inputs["L0_ffn_w_up"]).T),
        "wdT": np.ascontiguousarray(f32(full_inputs["L0_ffn_w_down"]).T),
    }
    m.update(host_constants())
    return m


def kernel(**inputs):
    nc, _ = build_program()
    in_maps = [make_inputs(inputs, c) for c in range(NCORE)]
    res = run_bass_kernel_spmd(nc, in_maps, core_ids=list(range(NCORE)))
    out = np.concatenate([np.asarray(r["out"]) for r in res.results], axis=0)
    return out.astype(np.float32)


# version 07
# speedup vs baseline: 1.0162x; 1.0162x over previous
import math
from contextlib import ExitStack

import numpy as np
import ml_dtypes

import concourse.bass as bass
import concourse.mybir as mybir
from concourse.bass_utils import run_bass_kernel_spmd

F32 = mybir.dt.float32
BF16 = mybir.dt.bfloat16
I32 = mybir.dt.int32
AF = mybir.ActivationFunctionType
ALU = mybir.AluOpType
AX = mybir.AxisListType

NCORE = 8
SEQ = 2
T = 2048
E = 1024
H, DH, DR, DKV = 16, 64, 32, 256
INNER = 4096
NST = 16
NTILE = SEQ * NST
SCALE = 1.0 / math.sqrt(DH + DR)
EPS = 1e-5
NEG = -1.0e30

ENGS = ("sync", "vector", "scalar", "tensor", "gpsimd")
DMAQ = ("sync", "scalar")
NDMASEM = 6

ALL_PHASES = ("p0", "p1", "p2", "p3preA", "p3preB", "p3a", "p3b")
P2C_PARTS = {"mm", "exp", "pool", "pv", "epi", "epi_all"}


class Prog:

    def __init__(self, nc):
        self.nc = nc
        self.sem_names = list(ENGS) + ["%s_dma%d" % (e, i) for e in DMAQ
                                       for i in range(NDMASEM)]
        self.cnt = {s: 0 for s in self.sem_names}
        self.q = {e: [] for e in ENGS}
        self.hw = {e: {s: 0 for s in self.sem_names} for e in ENGS}
        self.buf = {}
        self.dma_rr = {e: 0 for e in DMAQ}

    def op(self, eng, fn, reads=(), writes=(), dma=False):
        if dma:
            sem = "%s_dma%d" % (eng, self.dma_rr[eng])
            self.dma_rr[eng] = (self.dma_rr[eng] + 1) % NDMASEM
            inc = 16
        else:
            sem = eng
            inc = 1
        waits = {}
        if dma and self.cnt[sem] > 0:
            waits[sem] = self.cnt[sem]

        def dep(tok):
            if tok is None:
                return
            s, v = tok
            if s == "tensor" and eng == "tensor" and not dma:
                return
            if v > waits.get(s, 0):
                waits[s] = v

        for k in reads:
            st = self.buf.get(k)
            if st is not None:
                dep(st[0])
        for k in writes:
            st = self.buf.get(k)
            if st is not None:
                dep(st[0])
                for t in st[1]:
                    dep(t)
        tok = (sem, self.cnt[sem] + inc)
        self.cnt[sem] = tok[1]
        wl = []
        for s, v in waits.items():
            if v > self.hw[eng][s]:
                self.hw[eng][s] = v
                wl.append((s, v))
        self.q[eng].append({"waits": wl, "fn": fn, "sem": sem, "val": tok[1],
                            "dma": dma})
        for k in writes:
            self.buf[k] = [tok, []]
        for k in reads:
            st = self.buf.setdefault(k, [None, []])
            st[1].append(tok)
        return tok

    def barrier(self):
        snap = dict(self.cnt)
        for e in ENGS:
            wl = []
            for s, v in snap.items():
                if v > self.hw[e][s]:
                    self.hw[e][s] = v
                    wl.append((s, v))
            if wl:
                self.q[e].append({"waits": wl, "fn": None, "sem": e,
                                  "val": self.cnt[e], "dma": False})

    def build(self, ctx):
        sems = {s: ctx.enter_context(self.nc.semaphore("S_" + s))
                for s in self.sem_names}
        awaited = {s: set() for s in self.sem_names}
        for e in ENGS:
            for ent in self.q[e]:
                for s, v in ent["waits"]:
                    awaited[s].add(v)
        remap = {}
        for s in ENGS:
            remap[s] = {v: i + 1 for i, v in enumerate(sorted(awaited[s]))}

        blk = ctx.enter_context(self.nc.Block())

        def emit_engine(e):
            def body(eng):
                for ent in self.q[e]:
                    for s, v in ent["waits"]:
                        if s in remap:
                            eng.wait_ge(sems[s], remap[s][v])
                        else:
                            eng.wait_ge(sems[s], v)
                    if ent["fn"] is None:
                        continue
                    inst = ent["fn"](eng)
                    if ent["dma"]:
                        inst.then_inc(sems[ent["sem"]], 16)
                    elif ent["val"] in awaited[ent["sem"]]:
                        inst.then_inc(sems[ent["sem"]], 1)
            return body

        for e in ENGS:
            getattr(blk, e)(emit_engine(e))


def host_constants():
    c = {}
    c["c_ident"] = np.eye(128, dtype=ml_dtypes.bfloat16)
    r = np.arange(128)
    c["c_maskA"] = np.where(r[None, :] <= r[:, None], 0.0, NEG).astype(np.float32)
    c["c_maskT"] = np.where(r[:, None] <= r[None, :], 1.0, 0.0).astype(
        ml_dtypes.bfloat16)
    c["c_ones16"] = np.ones((128, 16), dtype=np.float32)
    c["c_onesrow"] = np.ones((1, 512), dtype=ml_dtypes.bfloat16)
    c["c_onesrowf"] = np.ones((1, 128), dtype=np.float32)
    c["c_zrow"] = np.zeros((1, 128), dtype=ml_dtypes.bfloat16)
    c["c_b1285"] = np.full((128, 1), 128.5, dtype=np.float32)
    c["c_b128t"] = np.full((128, 2048), 128.5, dtype=np.float32)
    c["c_zt"] = np.zeros((128, 1024), dtype=np.float32)
    c["c_identf"] = np.eye(128, dtype=np.float32)
    half = DR // 2
    inv = 1.0 / (10000.0 ** (np.arange(half, dtype=np.float64) / half))
    ang = np.arange(T, dtype=np.float64)[:, None] * inv[None, :]
    cos, sin = np.cos(ang), np.sin(ang)
    blk32_cos = np.concatenate([cos, cos], axis=1)
    blk32_sin = np.concatenate([sin, sin], axis=1)
    c["c_cosq"] = np.tile(blk32_cos * SCALE, (1, H)).astype(np.float32)
    c["c_sinq"] = np.tile(blk32_sin * SCALE, (1, H)).astype(np.float32)
    c["c_cosk"] = blk32_cos.astype(np.float32)
    c["c_sink"] = blk32_sin.astype(np.float32)
    return c


def build_program(dbg=(), phases=ALL_PHASES):
    nc = bass.Bass("TRN2", target_bir_lowering=False,
                   dynamic_dma_scratch_size=64)
    ctx = ExitStack()

    d = {}
    def din(name, shape, dtype):
        d[name] = nc.dram_tensor(name, shape, dtype, kind="ExternalInput").ap()
    din("x", [SEQ, T, E], F32)
    din("wq", [E, E], F32)
    din("wqr", [E, H * DR], F32)
    din("wkr", [E, DR], F32)
    din("wkvd", [E, DKV], F32)
    din("wku", [DKV, E], F32)
    din("wvu", [DKV, E], F32)
    din("wo", [E, E], F32)
    din("lobo", [1, H], F32)
    din("wgT", [E, INNER], F32)
    din("wuT", [E, INNER], F32)
    din("wdT", [INNER, E], F32)
    for name, arr_shape, dt in (
        ("c_ident", [128, 128], BF16), ("c_maskA", [128, 128], F32),
        ("c_maskT", [128, 128], BF16), ("c_ones16", [128, 16], F32),
        ("c_onesrow", [1, 512], BF16), ("c_onesrowf", [1, 128], F32),
        ("c_zrow", [1, 128], BF16), ("c_b1285", [128, 1], F32),
        ("c_b128t", [128, 2048], F32), ("c_zt", [128, 1024], F32),
        ("c_identf", [128, 128], F32),
        ("c_cosq", [T, H * DR], F32), ("c_sinq", [T, H * DR], F32),
        ("c_cosk", [T, DR], F32), ("c_sink", [T, DR], F32),
    ):
        din(name, arr_shape, dt)
    out_d = nc.dram_tensor("out", [SEQ, T, E], F32, kind="ExternalOutput").ap()
    attn_scr = nc.dram_tensor("attn_scr", [SEQ, NST, 128, E], F32,
                              kind="Internal").ap()
    x2_scr = nc.dram_tensor("x2_scr", [SEQ, NST, 128, E], F32,
                            kind="Internal").ap()
    xq_scr = nc.dram_tensor("xq_scr", [NTILE, 8, 128, 128], BF16,
                            kind="Internal").ap()
    gup_scr = nc.dram_tensor("gup_scr", [NTILE, 32, 128, 128], BF16,
                             kind="Internal").ap()

    p = Prog(nc)
    dbg_outs = {}

    def D(name, ap, reads):
        if name not in dbg:
            return
        o = nc.dram_tensor("dbg_" + name, list(ap.shape), ap.dtype,
                           kind="ExternalOutput").ap()
        dbg_outs[name] = o
        p.op("sync", lambda e, o=o, ap=ap: e.dma_start(out=o, in_=ap),
             reads=reads, dma=True)

    def sb(name, shape, dtype, scope=ctx):
        return scope.enter_context(nc.sbuf_tensor(name, shape, dtype))

    def sb_end(t_handle, name):
        m = nc.lookup_mloc(name)
        return m.addr + m.size() // 128

    def pad_to(scope, target, tag):
        probe = scope.enter_context(nc.sbuf_tensor("pp_" + tag, [128, 32],
                                                   mybir.dt.uint8))
        cur = nc.lookup_mloc("pp_" + tag).addr
        need = target - (cur + 32)
        assert need >= 0, (tag, cur, target)
        if need > 0:
            scope.enter_context(nc.sbuf_tensor("pd_" + tag, [128, need],
                                               mybir.dt.uint8))

    def pst(name, shape, dtype, scope=ctx):
        return scope.enter_context(nc.psum_tensor(name, shape, dtype))

    c_ident = sb("sb_c_ident", [128, 128], BF16)
    c_maskA = sb("sb_c_maskA", [128, 128], F32)
    c_maskT = sb("sb_c_maskT", [128, 128], BF16)
    c_ones16 = sb("sb_c_ones16", [128, 16], F32)
    c_onesrow = sb("sb_c_onesrow", [1, 512], BF16)
    c_onesrowf = sb("sb_c_onesrowf", [1, 128], F32)
    c_zrow = sb("sb_c_zrow", [1, 128], BF16)
    c_b1285 = sb("sb_c_b1285", [128, 1], F32)
    c_identf = sb("sb_c_identf", [128, 128], F32)
    lb_row = sb("lb_row", [1, 32], F32)
    x_st = sb("x_st", [128, E], F32)
    scr = sb("scr", [128, E], F32)
    st = sb("st", [128, 16], F32)
    st2 = sb("st2", [128, 16], F32)
    Cbc = sb("Cbc", [128, 256], F32)
    C256 = sb("C256", [1, 256], BF16)
    a127 = sb("a127", [128, NTILE], F32)
    fg_all = sb("fg_all", [128, NTILE], F32)
    fu_all = sb("fu_all", [128, NTILE], F32)
    fd_all = sb("fd_all", [128, NTILE], F32)
    swbc = {}
    mwbc = {}
    for w in ("g", "u", "dn"):
        swbc[w] = sb("swbc_" + w, [128, 1], F32)
        mwbc[w] = sb("mwbc_" + w, [128, 1], F32)

    psP = pst("psP", [128, 1024], F32)
    psS = pst("psS", [128, 512], F32)
    psT = pst("psT", [128, 1024], BF16)

    def dma(queue, out_ap, in_ap, reads=(), writes=()):
        return p.op(queue, lambda e, o=out_ap, i=in_ap: e.dma_start(out=o, in_=i),
                    reads=reads, writes=writes, dma=True)

    def ln_stats(src_ap, src_key):
        p.op("scalar", lambda e: e.activation(out=scr[:], in_=src_ap,
                                              func=AF.Square,
                                              accum_out=st[:, 1:2]),
             reads=[src_key],
             writes=[("scr",), ("scr2",), ("scr3",), ("st", 1)])
        p.op("vector", lambda e: e.reduce_sum(out=st[:, 0:1], in_=src_ap,
                                              axis=AX.X),
             reads=[src_key], writes=[("st", 0)])
        p.op("vector", lambda e: e.tensor_scalar_mul(st[:, 2:3], st[:, 0:1],
                                                     1.0 / E),
             reads=[("st", 0)], writes=[("st", 2)])
        p.op("vector", lambda e: e.tensor_scalar_mul(st[:, 3:4], st[:, 1:2],
                                                     1.0 / E),
             reads=[("st", 1)], writes=[("st", 3)])
        p.op("vector", lambda e: e.tensor_tensor(st[:, 4:5], st[:, 2:3],
                                                 st[:, 2:3], ALU.mult),
             reads=[("st", 2)], writes=[("st", 4)])
        p.op("vector", lambda e: e.scalar_tensor_tensor(
            out=st[:, 5:6], in0=st[:, 3:4], scalar=EPS, in1=st[:, 4:5],
            op0=ALU.add, op1=ALU.subtract),
             reads=[("st", 3), ("st", 4)], writes=[("st", 5)])
        p.op("scalar", lambda e: e.activation(out=st[:, 6:7], in_=st[:, 5:6],
                                              func=AF.Sqrt),
             reads=[("st", 5)], writes=[("st", 6)])
        p.op("vector", lambda e: e.reciprocal(st[:, 7:8], st[:, 6:7]),
             reads=[("st", 6)], writes=[("st", 7)])
        p.op("vector", lambda e: e.tensor_tensor(st[:, 8:9], st[:, 2:3],
                                                 st[:, 7:8], ALU.mult),
             reads=[("st", 2), ("st", 7)], writes=[("st", 8)])
        p.op("vector", lambda e: e.tensor_scalar_mul(st[:, 9:10], st[:, 8:9],
                                                     -1.0),
             reads=[("st", 8)], writes=[("st", 9)])

    def ln_apply(src_ap, src_key, dst_ap, dst_key):
        p.op("vector", lambda e: e.tensor_scalar(
            out=dst_ap, in0=src_ap, scalar1=st[:, 7:8], scalar2=st[:, 9:10],
            op0=ALU.mult, op1=ALU.add),
            reads=[src_key, ("st", 7), ("st", 9)], writes=[dst_key])

    def transpose8(src_sb, src_key, dst_sb, dst_key, n=8):
        for f in range(n):
            p.op("tensor", lambda e, f=f: e.transpose(
                out=psT[:, 128 * f:128 * f + 128],
                in_=src_sb[:, 128 * f:128 * f + 128], identity=c_ident[:]),
                reads=[src_key, ("ident",)], writes=[("psT",)])
        p.op("vector", lambda e: e.tensor_copy(out=dst_sb[:, 0:128 * n],
                                               in_=psT[:, 0:128 * n]),
             reads=[("psT",)], writes=[dst_key])

    def load_weight_bf16(dst_sb, dst_key, src_d, kdim, ndim, stg_sb, stg_key,
                         stage_cols):
        total = kdim * ndim
        done = 0
        while done < total:
            cols = min(stage_cols, total - done)
            k0, n_off = divmod(done, ndim)
            assert n_off == 0 and cols % ndim == 0
            kk = cols // ndim
            src = src_d.rearrange("(k p) n -> p k n", p=128)[:, k0:k0 + kk, :]
            dma("sync", stg_sb[:, 0:cols].rearrange("p (k n) -> p k n", k=kk),
                src, writes=[stg_key])
            p.op("vector",
                 lambda e, a=done, c=cols: e.tensor_copy(
                     out=dst_sb[:, a:a + c], in_=stg_sb[:, 0:c]),
                 reads=[stg_key], writes=[dst_key])
            done += cols

    if "p0" in phases:
        for name, sbuf in (("c_ident", c_ident), ("c_maskA", c_maskA),
                           ("c_maskT", c_maskT), ("c_ones16", c_ones16),
                           ("c_onesrow", c_onesrow), ("c_onesrowf", c_onesrowf),
                           ("c_zrow", c_zrow), ("c_b1285", c_b1285),
                           ("c_identf", c_identf)):
            dma("sync", sbuf[:], d[name], writes=[(name.replace("c_", ""),)])
        dma("sync", lb_row[0:1, 16:32], d["lobo"], writes=[("lb",)])
        p.op("scalar", lambda e: e.activation(out=lb_row[0:1, 0:16],
                                              in_=lb_row[0:1, 16:32],
                                              func=AF.Exp),
             reads=[("lb",)], writes=[("lbe",)])
        for h in range(H):
            p.op("vector", lambda e, h=h: e.tensor_scalar_mul(
                C256[0:1, 16 * h:16 * h + 16], c_onesrow[0:1, 0:16],
                lb_row[0:1, h:h + 1]),
                reads=[("lbe",), ("onesrow",)], writes=[("C256", h)])
        p.op("tensor", lambda e: e.matmul(out=psS[:, 0:256],
                                          lhsT=c_onesrow[0:1, 0:128],
                                          rhs=C256[0:1, 0:256],
                                          start=True, stop=True),
             reads=[("onesrow",)] + [("C256", h) for h in range(H)],
             writes=[("psS",)])
        p.op("vector", lambda e: e.tensor_copy(out=Cbc[:], in_=psS[:, 0:256]),
             reads=[("psS",)], writes=[("Cbc",)])

    att_scope = ExitStack()
    QS = sb("QS", [128, 8 * 4096], BF16, att_scope)
    KS = sb("KS", [128, 8 * 4096], BF16, att_scope)
    v65 = sb("v65", [128, NST * 1040], BF16, att_scope)
    h_bf = sb("h_bf", [128, E], BF16, att_scope)
    hfm = sb("hfm", [128, E], BF16, att_scope)
    pj_bf = sb("pj_bf", [128, E], BF16, att_scope)
    qr_bf = sb("qr_bf", [128, 512], BF16, att_scope)
    latfm = sb("latfm", [128, 256], BF16, att_scope)
    cosk_sb = sb("cosk_sb", [128, NST * DR], F32, att_scope)
    sink_sb = sb("sink_sb", [128, NST * DR], F32, att_scope)
    m_all = sb("m_all", [128, 256], F32, att_scope)
    Em_all = sb("Em_all", [128, 256], F32, att_scope)
    probsT = sb("probsT", [128, 1024], BF16, att_scope)
    mdiag = sb("mdiag", [128, 128], BF16, att_scope)
    CEm_all = sb("CEm_all", [128, 256], F32, att_scope)
    appiece = sb("appiece", [128, 512], F32, att_scope)

    def stack_ap(stacks, h, rows, col0, width):
        base = 2048 * h
        rr = {"all": (0, 128), "qc": (0, 64), "ro": (64, 96)}[rows]
        return stacks[rr[0]:rr[1], base + col0: base + col0 + width]

    if "p1" in phases:
        for stacks, key in ((QS, ("QS",)), (KS, ("KS",))):
            p.op("vector", lambda e, stacks=stacks: e.memset(
                stacks[64:128, :], 0.0), writes=[key])

    def emit_p1a(s):
        sc = ExitStack()
        wkvd = sb("wkvd_%d" % s, [128, 8 * DKV], BF16, sc)
        wku = sb("wku_%d" % s, [128, 2 * E], BF16, sc)
        wvu = sb("wvu_%d" % s, [128, 2 * E], BF16, sc)
        wkr = sb("wkr_%d" % s, [128, 8 * DR], BF16, sc)
        stg = sb("stg_%d" % s, [128, 2048], F32, sc)
        load_weight_bf16(wkvd, ("wkvd",), d["wkvd"], 8, DKV, stg, ("stg",), 2048)
        load_weight_bf16(wku, ("wku",), d["wku"], 2, E, stg, ("stg",), 2048)
        load_weight_bf16(wvu, ("wvu",), d["wvu"], 2, E, stg, ("stg",), 2048)
        load_weight_bf16(wkr, ("wkr",), d["wkr"], 8, DR, stg, ("stg",), 256)
        if s == 0:
            dma("sync", cosk_sb[:].rearrange("p (s j) -> p s j", s=NST),
                d["c_cosk"].rearrange("(s p) j -> p s j", p=128),
                writes=[("cosk",)])
            dma("sync", sink_sb[:].rearrange("p (s j) -> p s j", s=NST),
                d["c_sink"].rearrange("(s p) j -> p s j", p=128),
                writes=[("sink",)])
        for i in range(NST):
            dma("sync", x_st[:], d["x"][s, 128 * i:128 * i + 128, :],
                reads=[("x_st",)], writes=[("x_st",)])
            ln_stats(x_st[:], ("x_st",))
            ln_apply(x_st[:], ("x_st",), h_bf[:], ("h_bf",))
            transpose8(h_bf, ("h_bf",), hfm, ("hfm",))
            for k in range(8):
                p.op("tensor", lambda e, k=k: e.matmul(
                    out=psP[:, 0:256], lhsT=hfm[:, 128 * k:128 * k + 128],
                    rhs=wkvd[:, DKV * k:DKV * (k + 1)],
                    start=(k == 0), stop=(k == 7)),
                    reads=[("hfm",), ("wkvd",)], writes=[("psP",)])
            p.op("vector", lambda e: e.tensor_copy(out=pj_bf[:, 0:256],
                                                   in_=psP[:, 0:256]),
                 reads=[("psP",)], writes=[("pj",)])
            for f in range(2):
                p.op("tensor", lambda e, f=f: e.transpose(
                    out=psT[:, 128 * f:128 * f + 128],
                    in_=pj_bf[:, 128 * f:128 * f + 128], identity=c_ident[:]),
                    reads=[("pj",), ("ident",)], writes=[("psT",)])
            p.op("vector", lambda e: e.tensor_copy(out=latfm[:],
                                                   in_=psT[:, 0:256]),
                 reads=[("psT",)], writes=[("latfm",)])
            for c in range(2):
                for n in range(2):
                    p.op("tensor", lambda e, c=c, n=n: e.matmul(
                        out=psP[:, 512 * n:512 * (n + 1)],
                        lhsT=latfm[:, 128 * c:128 * (c + 1)],
                        rhs=wku[:, E * c + 512 * n:E * c + 512 * (n + 1)],
                        start=(c == 0), stop=(c == 1)),
                        reads=[("latfm",), ("wku",)], writes=[("psP",)])
            p.op("vector", lambda e: e.tensor_copy(out=pj_bf[:], in_=psP[:]),
                 reads=[("psP",)], writes=[("pj",)])
            for r in range(2):
                for hh in range(8):
                    h = 8 * r + hh
                    p.op("tensor", lambda e, h=h, hh=hh: e.transpose(
                        out=psT[0:64, 128 * hh:128 * hh + 128],
                        in_=pj_bf[:, 64 * h:64 * h + 64], identity=c_ident[:]),
                        reads=[("pj",), ("ident",)], writes=[("psT",)])
                p.op("vector", lambda e, i=i, r=r: e.tensor_copy(
                    out=KS[:].rearrange("p (h c) -> p h c", h=16)[
                        0:64, 8 * r:8 * r + 8, 128 * i:128 * i + 128],
                    in_=psT[0:64, :].rearrange("p (g c) -> p g c", g=8)),
                    reads=[("psT",)], writes=[("KS",)])
            for c in range(2):
                for n in range(2):
                    p.op("tensor", lambda e, c=c, n=n: e.matmul(
                        out=psP[:, 512 * n:512 * (n + 1)],
                        lhsT=latfm[:, 128 * c:128 * (c + 1)],
                        rhs=wvu[:, E * c + 512 * n:E * c + 512 * (n + 1)],
                        start=(c == 0), stop=(c == 1)),
                        reads=[("latfm",), ("wvu",)], writes=[("psP",)])
            v_i = v65[:, 1040 * i:1040 * (i + 1)].rearrange(
                "p (h c) -> p h c", h=H)
            p.op("vector", lambda e, v_i=v_i: e.tensor_copy(
                out=v_i[:, :, 0:64],
                in_=psP[:].rearrange("p (h c) -> p h c", h=H)),
                reads=[("psP",)], writes=[("v65",)])
            p.op("vector", lambda e, v_i=v_i: e.tensor_copy(
                out=v_i[:, :, 64:65],
                in_=c_ones16[:].rearrange("p (h c) -> p h c", h=H)),
                reads=[("ones16",)], writes=[("v65",)])
            for k in range(8):
                p.op("tensor", lambda e, k=k: e.matmul(
                    out=psS[:, 0:DR], lhsT=hfm[:, 128 * k:128 * k + 128],
                    rhs=wkr[:, DR * k:DR * (k + 1)],
                    start=(k == 0), stop=(k == 7)),
                    reads=[("hfm",), ("wkr",)], writes=[("psS",)])
            ck = cosk_sb[:, DR * i:DR * (i + 1)]
            sk = sink_sb[:, DR * i:DR * (i + 1)]
            p.op("vector", lambda e, ck=ck: e.tensor_tensor(
                scr[:, 0:DR], psS[:, 0:DR], ck, ALU.mult),
                reads=[("psS",), ("cosk",)], writes=[("scr",)])
            p.op("vector", lambda e, sk=sk: e.tensor_tensor(
                scr[:, 512:528], psS[:, 16:32], sk[:, 16:32], ALU.mult),
                reads=[("psS",), ("sink",)], writes=[("scr2",)])
            p.op("vector", lambda e, sk=sk: e.tensor_tensor(
                scr[:, 768:784], psS[:, 0:16], sk[:, 0:16], ALU.mult),
                reads=[("psS",), ("sink",)], writes=[("scr3",)])
            p.op("vector", lambda e: e.tensor_tensor(
                qr_bf[:, 0:16], scr[:, 0:16], scr[:, 512:528], ALU.subtract),
                reads=[("scr",), ("scr2",)], writes=[("qr",)])
            p.op("vector", lambda e: e.tensor_tensor(
                qr_bf[:, 16:32], scr[:, 16:32], scr[:, 768:784], ALU.add),
                reads=[("scr",), ("scr3",)], writes=[("qr",)])
            p.op("tensor", lambda e: e.transpose(
                out=psT[0:32, 0:128], in_=qr_bf[:, 0:DR],
                identity=c_ident[:]),
                reads=[("qr",), ("ident",)], writes=[("psT",)])
            for h in range(H):
                ap_dst = stack_ap(KS, h, "ro", 128 * i, 128)
                p.op("vector", lambda e, ap_dst=ap_dst: e.tensor_copy(
                    out=ap_dst, in_=psT[0:32, 0:128]),
                    reads=[("psT",)], writes=[("KS",)])
        sc.close()
        p.barrier()

    def emit_p1b(s):
        sc = ExitStack()
        wqsb = sb("wq_%d" % s, [128, 8 * E], BF16, sc)
        stg = sb("stgq_%d" % s, [128, 2048], F32, sc)
        load_weight_bf16(wqsb, ("wq",), d["wq"], 8, E, stg, ("stg",), 2048)
        for i in range(NST):
            dma("sync", x_st[:], d["x"][s, 128 * i:128 * i + 128, :],
                reads=[("x_st",)], writes=[("x_st",)])
            ln_stats(x_st[:], ("x_st",))
            ln_apply(x_st[:], ("x_st",), h_bf[:], ("h_bf",))
            D("x_%d_%d" % (s, i), x_st[:], [("x_st",)])
            D("h_%d_%d" % (s, i), h_bf[:], [("h_bf",)])
            transpose8(h_bf, ("h_bf",), hfm, ("hfm",))
            D("hfm_%d_%d" % (s, i), hfm[:], [("hfm",)])
            for k in range(8):
                for n in range(2):
                    p.op("tensor", lambda e, k=k, n=n: e.matmul(
                        out=psP[:, 512 * n:512 * (n + 1)],
                        lhsT=hfm[:, 128 * k:128 * k + 128],
                        rhs=wqsb[:, E * k + 512 * n:E * k + 512 * (n + 1)],
                        start=(k == 0), stop=(k == 7)),
                        reads=[("hfm",), ("wq",)], writes=[("psP",)])
            p.op("vector", lambda e: e.tensor_scalar_mul(pj_bf[:], psP[:],
                                                         SCALE),
                 reads=[("psP",)], writes=[("pj",)])
            D("qc_%d_%d" % (s, i), pj_bf[:], [("pj",)])
            D("wqdump_%d_%d" % (s, i), wqsb[:], [("wq",)])
            for r in range(2):
                for hh in range(8):
                    h = 8 * r + hh
                    p.op("tensor", lambda e, h=h, hh=hh: e.transpose(
                        out=psT[0:64, 128 * hh:128 * hh + 128],
                        in_=pj_bf[:, 64 * h:64 * h + 64], identity=c_ident[:]),
                        reads=[("pj",), ("ident",)], writes=[("psT",)])
                p.op("vector", lambda e, i=i, r=r: e.tensor_copy(
                    out=QS[:].rearrange("p (h c) -> p h c", h=16)[
                        0:64, 8 * r:8 * r + 8, 128 * i:128 * i + 128],
                    in_=psT[0:64, :].rearrange("p (g c) -> p g c", g=8)),
                    reads=[("psT",)], writes=[("QS",)])
        sc.close()
        p.barrier()
        sc = ExitStack()
        wqrsb = sb("wqr_%d" % s, [128, 8 * 512], BF16, sc)
        stg = sb("stgr_%d" % s, [128, 2048], F32, sc)
        cq = sb("cq_%d" % s, [128, 512], F32, sc)
        sq = sb("sq_%d" % s, [128, 512], F32, sc)
        load_weight_bf16(wqrsb, ("wqr",), d["wqr"], 8, 512, stg, ("stg",), 2048)
        for i in range(NST):
            dma("sync", x_st[:], d["x"][s, 128 * i:128 * i + 128, :],
                reads=[("x_st",)], writes=[("x_st",)])
            dma("sync", cq[:], d["c_cosq"][128 * i:128 * i + 128, :],
                reads=[("cq",)], writes=[("cq",)])
            dma("sync", sq[:], d["c_sinq"][128 * i:128 * i + 128, :],
                reads=[("sq",)], writes=[("sq",)])
            ln_stats(x_st[:], ("x_st",))
            ln_apply(x_st[:], ("x_st",), h_bf[:], ("h_bf",))
            transpose8(h_bf, ("h_bf",), hfm, ("hfm",))
            for k in range(8):
                p.op("tensor", lambda e, k=k: e.matmul(
                    out=psS[:, 0:512], lhsT=hfm[:, 128 * k:128 * k + 128],
                    rhs=wqrsb[:, 512 * k:512 * (k + 1)],
                    start=(k == 0), stop=(k == 7)),
                    reads=[("hfm",), ("wqr",)], writes=[("psS",)])
            ps3 = psS[:, 0:512].rearrange("p (h t j) -> p h t j", h=H, t=2)
            cq3 = cq[:].rearrange("p (h t j) -> p h t j", h=H, t=2)
            sq3 = sq[:].rearrange("p (h t j) -> p h t j", h=H, t=2)
            t13 = scr[:, 0:512].rearrange("p (h t j) -> p h t j", h=H, t=2)
            m13 = scr[:, 512:768].rearrange("p (h j) -> p h j", h=H)
            m23 = scr[:, 768:1024].rearrange("p (h j) -> p h j", h=H)
            qb3 = qr_bf[:].rearrange("p (h t j) -> p h t j", h=H, t=2)
            p.op("vector", lambda e: e.tensor_tensor(
                scr[:, 0:512], psS[:, 0:512], cq[:], ALU.mult),
                reads=[("psS",), ("cq",)], writes=[("scr",)])
            p.op("vector", lambda e: e.tensor_tensor(
                m13, ps3[:, :, 1, :], sq3[:, :, 1, :], ALU.mult),
                reads=[("psS",), ("sq",)], writes=[("scr2",)])
            p.op("vector", lambda e: e.tensor_tensor(
                m23, ps3[:, :, 0, :], sq3[:, :, 0, :], ALU.mult),
                reads=[("psS",), ("sq",)], writes=[("scr3",)])
            p.op("vector", lambda e: e.tensor_tensor(
                qb3[:, :, 0, :], t13[:, :, 0, :], m13, ALU.subtract),
                reads=[("scr",), ("scr2",)], writes=[("qr",)])
            p.op("vector", lambda e: e.tensor_tensor(
                qb3[:, :, 1, :], t13[:, :, 1, :], m23, ALU.add),
                reads=[("scr",), ("scr3",)], writes=[("qr",)])
            for r in range(2):
                for hh in range(8):
                    h = 8 * r + hh
                    p.op("tensor", lambda e, h=h, hh=hh: e.transpose(
                        out=psT[64:96, 128 * hh:128 * hh + 128],
                        in_=qr_bf[:, 32 * h:32 * h + 32], identity=c_ident[:]),
                        reads=[("qr",), ("ident",)], writes=[("psT",)])
                p.op("vector", lambda e, i=i, r=r: e.tensor_copy(
                    out=QS[:].rearrange("p (h c) -> p h c", h=16)[
                        64:96, 8 * r:8 * r + 8, 128 * i:128 * i + 128],
                    in_=psT[64:96, :].rearrange("p (g c) -> p g c", g=8)),
                    reads=[("psT",)], writes=[("QS",)])
        sc.close()
        p.barrier()

    def emit_p2a(s):
        sc = ExitStack()
        psA = pst("psA_%d" % s, [128, 2048], F32, sc)
        for h in range(H):
            for i in range(NST):
                ki = (i + 1) * 128
                nch = (ki + 511) // 512
                for c in range(nch):
                    w = min(512, ki - 512 * c)
                    p.op("tensor", lambda e, h=h, i=i, c=c, w=w: e.matmul(
                        out=psA[:, 512 * c:512 * c + w],
                        lhsT=stack_ap(QS, h, "all", 128 * i, 128),
                        rhs=stack_ap(KS, h, "all", 512 * c, w),
                        start=True, stop=True),
                        reads=[("QS",), ("KS",)], writes=[("psA",)])
                p.op("vector", lambda e, i=i: e.tensor_tensor(
                    scr[:, 0:128], psA[:, 128 * i:128 * i + 128], c_maskA[:],
                    ALU.add),
                    reads=[("psA",), ("maskA",)], writes=[("scr",)])
                col = 16 * h + i
                if i > 0:
                    p.op("vector", lambda e, i=i: e.reduce_max(
                        out=st2[:, 0:1], in_=psA[:, 0:128 * i], axis=AX.X),
                        reads=[("psA",)], writes=[("st2", 0)])
                    p.op("vector", lambda e: e.reduce_max(
                        out=st2[:, 1:2], in_=scr[:, 0:128], axis=AX.X),
                        reads=[("scr",)], writes=[("st2", 1)])
                    p.op("vector", lambda e, col=col: e.tensor_tensor(
                        m_all[:, col:col + 1], st2[:, 0:1], st2[:, 1:2],
                        ALU.max),
                        reads=[("st2", 0), ("st2", 1)], writes=[("m_all",)])
                else:
                    p.op("vector", lambda e, col=col: e.reduce_max(
                        out=m_all[:, col:col + 1], in_=scr[:, 0:128],
                        axis=AX.X),
                        reads=[("scr",)], writes=[("m_all",)])
        sc.close()
        p.barrier()
        p.op("scalar", lambda e: e.activation(out=Em_all[:], in_=m_all[:],
                                              func=AF.Exp),
             reads=[("m_all",)], writes=[("Em",)])
        p.op("vector", lambda e: e.tensor_tensor(CEm_all[:], Em_all[:],
                                                 Cbc[:], ALU.mult),
             reads=[("Em",), ("Cbc",)], writes=[("CEm",)])

    def emit_p2c(s):
        sc = ExitStack()
        psB = pst("psB_%d" % s, [128, 1024], F32, sc)
        psPV = pst("psPV_%d" % s, [128, 1024], F32, sc)
        for h in range(H):
            for HF in range(2):
                q_base = 1024 * HF
                for n in range(2):
                    p.op("tensor", lambda e, n=n: e.matmul(
                        out=psPV[:, 512 * n:512 * (n + 1)],
                        lhsT=c_zrow[0:1, :], rhs=c_onesrow[0:1, 0:512],
                        start=True, stop=False),
                        reads=[("zrow",), ("onesrow",)], writes=[("psPV",)])
                last_in_bank = {}
                for j in range(8 * HF + 8):
                    for sl in range(8):
                        if 8 * HF + sl >= j:
                            last_in_bank[sl // 4] = (j, sl)
                for j in range(8 * HF + 8):
                    q0 = max(q_base, 128 * j)
                    a = q0 - q_base
                    for seg0, seg1 in ((0, 512), (512, 1024)):
                        lo = max(a, seg0)
                        if lo >= seg1 or "mm" not in P2C_PARTS:
                            continue
                        p.op("tensor",
                             lambda e, h=h, j=j, lo=lo, seg1=seg1,
                             q_base=q_base: e.matmul(
                                 out=psB[:, lo:seg1],
                                 lhsT=stack_ap(KS, h, "all", 128 * j, 128),
                                 rhs=stack_ap(QS, h, "all", q_base + lo,
                                              seg1 - lo),
                                 start=True, stop=True),
                             reads=[("KS",), ("QS",)], writes=[("psB",)])
                    if "exp" in P2C_PARTS:
                        p.op("scalar", lambda e, a=a: e.activation(
                            out=probsT[:, a:1024], in_=psB[:, a:1024],
                            func=AF.Exp),
                            reads=[("psB",)], writes=[("probsT",)])
                    if 128 * j >= q_base and "pool" in P2C_PARTS:
                        p.op("gpsimd", lambda e, a=a: e.tensor_tensor(
                            mdiag[:], probsT[:, a:a + 128],
                            c_maskT[:], ALU.mult),
                            reads=[("probsT",), ("maskT",)],
                            writes=[("mdiag",)])
                    D("pb_%d_%d_%d_%d" % (s, h, HF, j), probsT[:],
                      [("probsT",)])
                    for sl in range(8):
                        sg = 8 * HF + sl
                        if sg < j:
                            continue
                        stop_f = (last_in_bank[sl // 4] == (j, sl))
                        if "pv" not in P2C_PARTS:
                            continue
                        if sg == j:
                            lhs_ap = mdiag[:]
                            lhs_key = ("mdiag",)
                        else:
                            lhs_ap = probsT[:, 128 * sg - q_base:
                                            128 * sg - q_base + 128]
                            lhs_key = ("probsT",)
                        p.op("tensor",
                             lambda e, h=h, j=j, sl=sl, lhs_ap=lhs_ap,
                             stop_f=stop_f: e.matmul(
                                 out=psPV[:, 128 * sl:128 * sl + 65],
                                 lhsT=lhs_ap,
                                 rhs=v65[:, 1040 * j + 65 * h:
                                         1040 * j + 65 * h + 65],
                                 start=False, stop=stop_f),
                             reads=[lhs_key, ("v65",)],
                             writes=[("psPV",)])
                if "epi" not in P2C_PARTS:
                    continue
                pv3 = psPV[:].rearrange("p (s c) -> p s c", s=8)
                if "epi_stt" in P2C_PARTS or "epi_all" in P2C_PARTS:
                    p.op("vector", lambda e, h=h, HF=HF, pv3=pv3:
                         e.scalar_tensor_tensor(
                             out=st2[:, 0:8], in0=pv3[:, :, 64:65].rearrange(
                                 "p s c -> p (s c)"),
                             scalar=1.0,
                             in1=CEm_all[:, 16 * h + 8 * HF:
                                         16 * h + 8 * HF + 8],
                             op0=ALU.mult, op1=ALU.add),
                         reads=[("psPV",), ("CEm",)], writes=[("st2", 0)])
                if "epi_recip" in P2C_PARTS or "epi_all" in P2C_PARTS:
                    p.op("vector", lambda e: e.reciprocal(st2[:, 8:16],
                                                          st2[:, 0:8]),
                         reads=[("st2", 0)], writes=[("st2", 8)])
                if "epi_ts" in P2C_PARTS or "epi_all" in P2C_PARTS:
                    for sl in range(8):
                        p.op("vector", lambda e, sl=sl: e.tensor_scalar(
                            out=appiece[:, 64 * sl:64 * sl + 64],
                            in0=psPV[:, 128 * sl:128 * sl + 64],
                            scalar1=st2[:, 8 + sl:9 + sl], scalar2=None,
                            op0=ALU.mult),
                            reads=[("psPV",), ("st2", 8)], writes=[("app",)])
                if "epi_dma" in P2C_PARTS or "epi_all" in P2C_PARTS:
                    dst = attn_scr[s, 8 * HF:8 * HF + 8, :,
                                   64 * h:64 * h + 64]
                    dst = dst.rearrange("s p c -> p s c")
                    src = appiece[:].rearrange("p (s c) -> p s c", s=8)
                    dma("scalar", dst, src, reads=[("app",)],
                        writes=[("attn_scr", s, 8 * HF + k)
                                for k in range(8)])
        sc.close()
        p.barrier()

    if "p1" in phases or "p2" in phases:
        for s in range(SEQ):
            if "p1" in phases:
                emit_p1a(s)
                emit_p1b(s)
                D("QS%d" % s, QS[:], [("QS",)])
                D("KS%d" % s, KS[:], [("KS",)])
                D("v65_%d" % s, v65[:], [("v65",)])
            if "p2" in phases or "p2a_only" in phases:
                emit_p2a(s)
                D("m%d" % s, m_all[:], [("m_all",)])
                D("CEm%d" % s, CEm_all[:], [("CEm",)])
                if "p2" in phases:
                    emit_p2c(s)
    att_scope.close()
    p.barrier()
    if "p2" in phases:
        D("attn", attn_scr,
          [("attn_scr", s, i) for s in range(SEQ) for i in range(NST)])

    if "p3preA" in phases:
        sc = ExitStack()
        wosb = sb("wo_sb", [128, 8 * E], BF16, sc)
        stg = sb("stg_o", [128, 2048], F32, sc)
        ap_ld = sb("ap_ld", [128, E], F32, sc)
        apb = sb("apb", [128, E], BF16, sc)
        afm = sb("afm", [128, E], BF16, sc)
        x2_sb = sb("x2_sb", [128, E], F32, sc)
        h2_sb = sb("h2_sb", [128, E], F32, sc)
        u_sb = sb("u_sb", [128, E], F32, sc)
        i_sb = sb("i_sb", [128, E], I32, sc)
        lt_sb = sb("lt_sb", [128, E], F32, sc)
        xq_sb = sb("xq_sb", [128, E], BF16, sc)
        xqfm = sb("xqfm", [128, E], BF16, sc)
        a_cb128 = sb("cb128_a", [128, E], F32, sc)
        a_czt = sb("czt_a", [128, 1024], F32, sc)
        dma("sync", a_cb128[:], d["c_b128t"][:, 0:E], writes=[("cb128",)])
        dma("sync", a_czt[:], d["c_zt"], writes=[("czt",)])
        load_weight_bf16(wosb, ("wo",), d["wo"], 8, E, stg, ("stg",), 2048)
        for s in range(SEQ):
            for i in range(NST):
                t = 16 * s + i
                dma("sync", ap_ld[:], attn_scr[s, i],
                    reads=[("ap_ld",), ("attn_scr", s, i)], writes=[("ap_ld",)])
                p.op("vector", lambda e: e.tensor_copy(out=apb[:],
                                                       in_=ap_ld[:]),
                     reads=[("ap_ld",)], writes=[("apb",)])
                transpose8(apb, ("apb",), afm, ("afm",))
                for k in range(8):
                    for n in range(2):
                        p.op("tensor", lambda e, k=k, n=n: e.matmul(
                            out=psP[:, 512 * n:512 * (n + 1)],
                            lhsT=afm[:, 128 * k:128 * k + 128],
                            rhs=wosb[:, E * k + 512 * n:E * k + 512 * (n + 1)],
                            start=(k == 0), stop=(k == 7)),
                            reads=[("afm",), ("wo",)], writes=[("psP",)])
                dma("sync", x_st[:], d["x"][s, 128 * i:128 * i + 128, :],
                    reads=[("x_st",)], writes=[("x_st",)])
                p.op("vector", lambda e: e.tensor_tensor(
                    x2_sb[:], psP[:], x_st[:], ALU.add),
                    reads=[("psP",), ("x_st",)], writes=[("x2",)])
                dma("scalar", x2_scr[s, i], x2_sb[:], reads=[("x2",)],
                    writes=[("x2_scr", s, i)])
                ln_stats(x2_sb[:], ("x2",))
                ln_apply(x2_sb[:], ("x2",), h2_sb[:], ("h2",))
                p.op("vector", lambda e: e.tensor_reduce(
                    out=st[:, 10:11], in_=h2_sb[:], axis=AX.X, op=ALU.max,
                    apply_absolute_value=True),
                    reads=[("h2",)], writes=[("st", 10)])
                p.op("vector", lambda e: e.tensor_scalar_max(
                    st[:, 11:12], st[:, 10:11], EPS),
                    reads=[("st", 10)], writes=[("st", 11)])
                p.op("vector", lambda e: e.reciprocal(st[:, 12:13],
                                                      st[:, 11:12]),
                     reads=[("st", 11)], writes=[("st", 12)])
                p.op("vector", lambda e: e.tensor_scalar_mul(
                    st[:, 13:14], st[:, 12:13], 127.0),
                    reads=[("st", 12)], writes=[("st", 13)])
                p.op("vector", lambda e, t=t: e.tensor_scalar_mul(
                    a127[:, t:t + 1], st[:, 11:12], 1.0 / 127.0),
                    reads=[("st", 11)], writes=[("a127", t)])
                p.op("vector", lambda e: e.scalar_tensor_tensor(
                    out=u_sb[:], in0=h2_sb[:], scalar=st[:, 13:14],
                    in1=a_cb128[:, 0:E], op0=ALU.mult, op1=ALU.add),
                    reads=[("h2",), ("st", 13), ("cb128",)], writes=[("u",)])
                p.op("vector", lambda e: e.tensor_tensor(
                    i_sb[:], u_sb[:], a_czt[:], ALU.add),
                    reads=[("u",), ("czt",)], writes=[("i32",)])
                p.op("vector", lambda e: e.tensor_tensor(
                    lt_sb[:], u_sb[:], i_sb[:], ALU.is_lt),
                    reads=[("u",), ("i32",)], writes=[("lt",)])
                p.op("vector", lambda e: e.scalar_tensor_tensor(
                    out=xq_sb[:], in0=i_sb[:], scalar=128.0, in1=lt_sb[:],
                    op0=ALU.subtract, op1=ALU.subtract),
                    reads=[("i32",), ("lt",)], writes=[("xq",)])
                transpose8(xq_sb, ("xq",), xqfm, ("xqfm",))
                dma("scalar",
                    xq_scr[t].rearrange("c p n -> p c n"),
                    xqfm[:].rearrange("p (c n) -> p c n", c=8),
                    reads=[("xqfm",)], writes=[("xq_scr", t)])
        sc.close()
        p.barrier()
        D("x2", x2_scr,
          [("x2_scr", s, i) for s in range(SEQ) for i in range(NST)])
        D("xq", xq_scr, [("xq_scr", t) for t in range(NTILE)])

    def quant_weight_pieces(dst_sb, dst_key, src_d, n_pieces, piece_cols,
                            wname, stg, u2, i2, lt2, cb128, czt,
                            dst_off=0):
        src3 = src_d.rearrange("(k p) n -> p k n", p=128)
        ndim = src_d.shape[1]
        per_k = ndim
        segs = dst_sb if isinstance(dst_sb, list) else [
            (dst_sb, dst_key, dst_sb.shape[1])]
        for pc in range(n_pieces):
            c0 = pc * piece_cols
            k0, n_off = divmod(c0, per_k)
            assert n_off % piece_cols == 0 or per_k % piece_cols == 0
            if per_k >= piece_cols:
                src = src3[:, k0:k0 + 1, n_off:n_off + piece_cols]
            else:
                kk = piece_cols // per_k
                src = src3[:, k0:k0 + kk, :]
            dma("sync", stg[:, 0:piece_cols].rearrange(
                "p (k n) -> p k n", n=min(per_k, piece_cols)), src,
                reads=[("stg3",)], writes=[("stg3",)])
            p.op("vector", lambda e: e.tensor_scalar(
                out=u2[:, 0:piece_cols], in0=stg[:, 0:piece_cols],
                scalar1=swbc[wname][:, 0:1], scalar2=None, op0=ALU.mult),
                reads=[("stg3",), ("swmw", wname, "s")],
                writes=[("u2",)])
            p.op("vector", lambda e: e.tensor_scalar(
                out=lt2[:, 0:piece_cols], in0=u2[:, 0:piece_cols],
                scalar1=0.5, scalar2=None, op0=ALU.is_gt),
                reads=[("u2",)], writes=[("lt2",)])
            p.op("vector", lambda e: e.tensor_scalar(
                out=i2[:, 0:piece_cols].bitcast(F32),
                in0=u2[:, 0:piece_cols],
                scalar1=-0.5, scalar2=None, op0=ALU.is_lt),
                reads=[("u2",)], writes=[("i2",)])
            cc = c0
            for seg_sb, seg_key, seg_n in segs:
                if cc < seg_n:
                    break
                cc -= seg_n
            assert cc + piece_cols <= seg_n
            p.op("vector", lambda e, seg_sb=seg_sb, cc=cc: e.tensor_tensor(
                seg_sb[:, cc:cc + piece_cols], lt2[:, 0:piece_cols],
                i2[:, 0:piece_cols].bitcast(F32), ALU.subtract),
                reads=[("lt2",), ("i2",)], writes=[seg_key])

    if "p3preB" in phases:
        sc = ExitStack()
        stgbuf = (sb("stg_s", [128, 4096], F32, sc),
                  sb("stg_s2", [128, 4096], F32, sc))
        accb = sb("accb", [128, 64], F32, sc)
        prow = sb("prow", [128, 16], F32, sc)
        pb = 0
        for wname, src_d, nch in (("g", d["wgT"], 8), ("u", d["wuT"], 8),
                                  ("dn", d["wdT"], 8)):
            for c in range(nch):
                stg = stgbuf[pb]
                skey = ("stg4", pb)
                if wname == "dn":
                    srcap = src_d.rearrange("(k p) n -> p k n", p=128)[
                        :, 4 * c:4 * c + 4, :]
                    dmain = stg[:].rearrange("p (k n) -> p k n", k=4)
                else:
                    srcap = src_d.rearrange("(k p) n -> p k n", p=128)[
                        :, c:c + 1, :]
                    dmain = stg[:].rearrange("p (k n) -> p k n", k=1)
                dma("sync" if pb == 0 else "scalar", dmain, srcap,
                    reads=[skey], writes=[skey])
                pb ^= 1
                p.op("vector", lambda e, c=c, stg=stg: e.tensor_reduce(
                    out=accb[:, 8 * c:8 * c + 8],
                    in_=stg[:].rearrange("p (g n) -> p g n", g=8),
                    axis=AX.X, op=ALU.add, apply_absolute_value=True),
                    reads=[skey], writes=[("accb", c)])
            p.op("vector", lambda e: e.tensor_reduce(
                out=prow[:, 0:1], in_=accb[:], axis=AX.X, op=ALU.add),
                reads=[("accb", c) for c in range(8)], writes=[("prow",)])
            p.op("tensor", lambda e: e.transpose(
                out=psS[0:1, 0:128], in_=prow[:, 0:1],
                identity=c_identf[:]),
                reads=[("prow",), ("identf",)], writes=[("psS",)])
            p.op("vector", lambda e: e.tensor_reduce(
                out=prow[0:1, 0:8], in_=psS[0:1, 0:128].rearrange(
                    "p (g n) -> p g n", g=8), axis=AX.X, op=ALU.add),
                reads=[("psS",)], writes=[("prow2",)])
            p.op("vector", lambda e: e.tensor_reduce(
                out=st2[0:1, 0:1], in_=prow[0:1, 0:8], axis=AX.X, op=ALU.add),
                reads=[("prow2",)], writes=[("st2", 0)])
            p.op("vector", lambda e: e.tensor_scalar_mul(
                st2[0:1, 2:3], st2[0:1, 0:1], 1.0 / (INNER * E)),
                reads=[("st2", 0)], writes=[("st2", 2)])
            p.op("vector", lambda e: e.tensor_scalar_max(
                st2[0:1, 3:4], st2[0:1, 2:3], EPS),
                reads=[("st2", 2)], writes=[("st2", 3)])
            p.op("vector", lambda e: e.reciprocal(st2[0:1, 4:5],
                                                  st2[0:1, 3:4]),
                 reads=[("st2", 3)], writes=[("st2", 4)])
            for srccol, dst, role in ((3, mwbc[wname], "m"),
                                      (4, swbc[wname], "s")):
                p.op("tensor", lambda e, srccol=srccol: e.matmul(
                    out=psS[:, 4:5], lhsT=c_onesrowf[0:1, :],
                    rhs=st2[0:1, srccol:srccol + 1], start=True, stop=True),
                    reads=[("st2", srccol), ("onesrowf",)], writes=[("psS",)])
                p.op("vector", lambda e, dst=dst: e.tensor_copy(
                    out=dst[:], in_=psS[:, 4:5]),
                    reads=[("psS",)], writes=[("swmw", wname, role)])
        p.op("vector", lambda e: e.tensor_scalar(
            out=fg_all[:], in0=a127[:], scalar1=mwbc["g"][:, 0:1],
            scalar2=None, op0=ALU.mult),
            reads=[("swmw", "g", "m")] + [("a127", t) for t in
                                               range(NTILE)],
            writes=[("fg",)])
        p.op("vector", lambda e: e.tensor_scalar(
            out=fu_all[:], in0=a127[:], scalar1=mwbc["u"][:, 0:1],
            scalar2=None, op0=ALU.mult),
            reads=[("swmw", "u", "m")] + [("a127", t) for t in
                                               range(NTILE)],
            writes=[("fu",)])
        D("a127", a127[:], [("a127", t) for t in range(NTILE)])
        D("fg", fg_all[:], [("fg",)])
        D("fu", fu_all[:], [("fu",)])
        sc.close()
        p.barrier()

    if "p3a" in phases:
        sc = ExitStack()
        u2 = sb("u2", [128, 1024], F32, sc)
        i2 = sb("i2", [128, 1024], I32, sc)
        lt2 = sb("lt2", [128, 1024], F32, sc)
        cb128 = sb("cb128_b", [128, 1024], F32, sc)
        czt = sb("czt_b", [128, 1024], F32, sc)
        u2_addr = nc.lookup_mloc("u2").addr
        assert u2_addr < 65536
        wqgL = sb("wqgL", [128, 3 * INNER], BF16, sc)
        pad_to(sc, 63456, "a0")
        xqfm2 = sb("xqfm2", [128, E], BF16, sc)
        pad_to(sc, 65536, "a0b")
        stg3 = sb("stg3", [128, 2048], F32, sc)
        gates = sb("gates", [128, 1024], F32, sc)
        pad_to(sc, u2_addr + 65536, "a1")
        sb("sacr_a", [128, 12288], mybir.dt.uint8, sc)
        gup = sb("gup", [128, INNER], F32, sc)
        gupq = sb("gupq", [128, INNER], BF16, sc)
        gqf = sb("gqf", [128, 1024], BF16, sc)
        wqgM = sb("wqgM", [128, INNER], BF16, sc)
        pad_to(sc, 131040, "a2")
        wqgB = sb("wqgB", [128, 4 * INNER], BF16, sc)
        wqu = sb("wqu", [128, 8 * INNER], BF16, sc)
        assert sb_end(wqu, "wqu") <= 229344, sb_end(wqu, "wqu")
        assert nc.lookup_mloc("xqfm2").addr == 63456
        for nm in ("gup", "gupq", "gqf", "gates", "stg3"):
            mm_ = nc.lookup_mloc(nm)
            assert mm_.addr >= 65536 and \
                mm_.addr + mm_.size() // 128 <= 131072, nm
        psU = pst("psU", [128, 1024], F32, sc)
        psG = pst("psG", [128, 1024], F32, sc)
        dma("sync", cb128[:], d["c_b128t"][:, 0:1024], writes=[("cb128",)])
        dma("sync", czt[:], d["c_zt"], writes=[("czt",)])
        wqg_segs = [(wqgL, ("wqgL",), 3 * INNER), (wqgM, ("wqgM",), INNER),
                    (wqgB, ("wqgB",), 4 * INNER)]
        quant_weight_pieces(wqg_segs, None, d["wgT"], 32, 1024, "g",
                            stg3, u2, i2, lt2, cb128, czt)
        quant_weight_pieces(wqu, ("wqu",), d["wuT"], 32, 1024, "u",
                            stg3, u2, i2, lt2, cb128, czt)

        def wqg_ap(k, lo, hi):
            if k < 3:
                return wqgL[:, INNER * k + lo:INNER * k + hi], ("wqgL",)
            if k < 4:
                return wqgM[:, lo:hi], ("wqgM",)
            return wqgB[:, INNER * (k - 4) + lo:INNER * (k - 4) + hi], ("wqgB",)

        for t in range(NTILE):
            dma("sync", xqfm2[:].rearrange("p (c n) -> p c n", c=8),
                xq_scr[t].rearrange("c p n -> p c n"),
                reads=[("xqfm2",), ("xq_scr", t)], writes=[("xqfm2",)])
            for Q in range(4):
                for k in range(8):
                    for n in range(2):
                        gap, gkey = wqg_ap(k, 1024 * Q + 512 * n,
                                           1024 * Q + 512 * (n + 1))
                        p.op("tensor", lambda e, k=k, n=n, gap=gap: e.matmul(
                            out=psG[:, 512 * n:512 * (n + 1)],
                            lhsT=xqfm2[:, 128 * k:128 * k + 128],
                            rhs=gap, start=(k == 0), stop=(k == 7)),
                            reads=[("xqfm2",), gkey], writes=[("psG",)])
                for k in range(8):
                    for n in range(2):
                        p.op("tensor", lambda e, k=k, n=n, Q=Q: e.matmul(
                            out=psU[:, 512 * n:512 * (n + 1)],
                            lhsT=xqfm2[:, 128 * k:128 * k + 128],
                            rhs=wqu[:, INNER * k + 1024 * Q + 512 * n:
                                    INNER * k + 1024 * Q + 512 * (n + 1)],
                            start=(k == 0), stop=(k == 7)),
                            reads=[("xqfm2",), ("wqu",)], writes=[("psU",)])
                p.op("scalar", lambda e, t=t: e.activation(
                    out=gates[:], in_=psG[:], func=AF.Silu,
                    scale=fg_all[:, t:t + 1]),
                    reads=[("psG",), ("fg",)], writes=[("gates",)])
                p.op("vector", lambda e, t=t, Q=Q: e.scalar_tensor_tensor(
                    out=gup[:, 1024 * Q:1024 * (Q + 1)], in0=psU[:],
                    scalar=fu_all[:, t:t + 1], in1=gates[:],
                    op0=ALU.mult, op1=ALU.mult),
                    reads=[("psU",), ("fu",), ("gates",)],
                    writes=[("gup", Q)])
                p.op("vector", lambda e, Q=Q: e.tensor_reduce(
                    out=st2[:, Q:Q + 1],
                    in_=gup[:, 1024 * Q:1024 * (Q + 1)], axis=AX.X,
                    op=ALU.max, apply_absolute_value=True),
                    reads=[("gup", Q)], writes=[("st2", Q)])
            p.op("vector", lambda e: e.tensor_reduce(
                out=st2[:, 4:5], in_=st2[:, 0:4], axis=AX.X, op=ALU.max),
                reads=[("st2", 0), ("st2", 1), ("st2", 2), ("st2", 3)],
                writes=[("st2", 4)])
            p.op("vector", lambda e: e.tensor_scalar_max(
                st2[:, 5:6], st2[:, 4:5], EPS),
                reads=[("st2", 4)], writes=[("st2", 5)])
            p.op("vector", lambda e: e.reciprocal(st2[:, 6:7], st2[:, 5:6]),
                 reads=[("st2", 5)], writes=[("st2", 6)])
            p.op("vector", lambda e: e.tensor_scalar_mul(
                st2[:, 7:8], st2[:, 6:7], 127.0),
                reads=[("st2", 6)], writes=[("st2", 7)])
            p.op("vector", lambda e: e.tensor_scalar_mul(
                st2[:, 9:10], st2[:, 5:6], 1.0 / 127.0),
                reads=[("st2", 5)], writes=[("st2", 9)])
            p.op("vector", lambda e, t=t: e.tensor_scalar(
                out=fd_all[:, t:t + 1], in0=st2[:, 9:10],
                scalar1=mwbc["dn"][:, 0:1], scalar2=None, op0=ALU.mult),
                reads=[("st2", 9), ("swmw", "dn", "m")],
                writes=[("fd", t)])
            for Q in range(4):
                p.op("vector", lambda e, Q=Q: e.scalar_tensor_tensor(
                    out=u2[:, 0:1024], in0=gup[:, 1024 * Q:1024 * (Q + 1)],
                    scalar=st2[:, 7:8], in1=cb128[:, 0:1024],
                    op0=ALU.mult, op1=ALU.add),
                    reads=[("gup", Q), ("st2", 7), ("cb128",)],
                    writes=[("u2",)])
                p.op("vector", lambda e: e.tensor_tensor(
                    i2[:, 0:1024], u2[:, 0:1024], czt[:, 0:1024], ALU.add),
                    reads=[("u2",), ("czt",)], writes=[("i2",)])
                p.op("vector", lambda e: e.tensor_tensor(
                    lt2[:, 0:1024], u2[:, 0:1024], i2[:, 0:1024], ALU.is_lt),
                    reads=[("u2",), ("i2",)], writes=[("lt2",)])
                p.op("vector", lambda e, Q=Q: e.scalar_tensor_tensor(
                    out=gupq[:, 1024 * Q:1024 * (Q + 1)], in0=i2[:, 0:1024],
                    scalar=128.0, in1=lt2[:, 0:1024],
                    op0=ALU.subtract, op1=ALU.subtract),
                    reads=[("i2",), ("lt2",)], writes=[("gupq", Q)])
            D("wq_t%d" % t, wqu[:], [("wqu",)])
            D("gup_%d" % t, gup[:], [("gup", 0), ("gup", 1), ("gup", 2),
                                     ("gup", 3)])
            D("gupq_%d" % t, gupq[:], [("gupq", 0), ("gupq", 1),
                                       ("gupq", 2), ("gupq", 3)])
            for r in range(4):
                for f in range(8):
                    p.op("tensor", lambda e, r=r, f=f: e.transpose(
                        out=psT[:, 128 * f:128 * f + 128],
                        in_=gupq[:, 1024 * r + 128 * f:
                                 1024 * r + 128 * f + 128],
                        identity=c_ident[:]),
                        reads=[("gupq", r), ("ident",)], writes=[("psT",)])
                p.op("vector", lambda e: e.tensor_copy(out=gqf[:],
                                                       in_=psT[:]),
                     reads=[("psT",)], writes=[("gqf",)])
                dma("scalar",
                    gup_scr[t, 8 * r:8 * r + 8].rearrange("c p n -> p c n"),
                    gqf[:].rearrange("p (c n) -> p c n", c=8),
                    reads=[("gqf",)], writes=[("gup_scr", t, r)])
        sc.close()
        p.barrier()

    if "p3b" in phases:
        sc = ExitStack()
        b_u2 = sb("u2b", [128, 1024], F32, sc)
        b_i2 = sb("i2b", [128, 1024], I32, sc)
        b_lt2 = sb("lt2b", [128, 1024], F32, sc)
        b_cb128 = sb("cb128_c", [128, 1024], F32, sc)
        b_czt = sb("czt_c", [128, 1024], F32, sc)
        u2_addr = nc.lookup_mloc("u2b").addr
        assert u2_addr < 65536
        wqdL = sb("wqdL", [128, 9 * E], BF16, sc)
        gqfm = sb("gqfm", [128, INNER], BF16, sc)
        x2b = sb("x2b", [128, E], F32, sc)
        pad_to(sc, 65536, "b0b")
        b_stg3 = sb("stg3b", [128, 2048], F32, sc)
        pad_to(sc, u2_addr + 65536, "b1")
        sb("sacr_b", [128, 12288], mybir.dt.uint8, sc)
        out_sb = sb("out_sb", [128, E], F32, sc)
        wqdM = sb("wqdM", [128, 11 * E], BF16, sc)
        pad_to(sc, 131040, "b2")
        wqdB = sb("wqdB", [128, 12 * E], BF16, sc)
        assert nc.lookup_mloc("gqfm").addr == 53216
        for nm in ("out_sb", "stg3b", "wqdM"):
            mm_ = nc.lookup_mloc(nm)
            assert mm_.addr >= 65536 and \
                mm_.addr + mm_.size() // 128 <= 65536 + 53216, nm
        dma("sync", b_cb128[:], d["c_b128t"][:, 0:1024], writes=[("cb128",)])
        dma("sync", b_czt[:], d["c_zt"], writes=[("czt",)])
        wqd_segs = [(wqdL, ("wqdL",), 9 * E), (wqdM, ("wqdM",), 11 * E),
                    (wqdB, ("wqdB",), 12 * E)]
        quant_weight_pieces(wqd_segs, None, d["wdT"], 32, 1024, "dn",
                            b_stg3, b_u2, b_i2, b_lt2, b_cb128, b_czt)

        def wqd_ap(k, lo, hi):
            if k < 9:
                return wqdL[:, E * k + lo:E * k + hi], ("wqdL",)
            if k < 20:
                return wqdM[:, E * (k - 9) + lo:E * (k - 9) + hi], ("wqdM",)
            return wqdB[:, E * (k - 20) + lo:E * (k - 20) + hi], ("wqdB",)

        for t in range(NTILE):
            s, i = divmod(t, NST)
            dma("sync", gqfm[:].rearrange("p (c n) -> p c n", c=32),
                gup_scr[t].rearrange("c p n -> p c n"),
                reads=[("gqfm",)] + [("gup_scr", t, r) for r in range(4)],
                writes=[("gqfm",)])
            dma("sync", x2b[:], x2_scr[s, i],
                reads=[("x2b",), ("x2_scr", s, i)], writes=[("x2b",)])
            if t == 8:
                D("x2b8", x2b[:], [("x2b",)])
                D("gqfm8", gqfm[:], [("gqfm",)])
            for k in range(32):
                for n in range(2):
                    dap, dkey = wqd_ap(k, 512 * n, 512 * (n + 1))
                    p.op("tensor", lambda e, k=k, n=n, dap=dap: e.matmul(
                        out=psP[:, 512 * n:512 * (n + 1)],
                        lhsT=gqfm[:, 128 * k:128 * k + 128],
                        rhs=dap, start=(k == 0), stop=(k == 31)),
                        reads=[("gqfm",), dkey], writes=[("psP",)])
            p.op("vector", lambda e, t=t: e.scalar_tensor_tensor(
                out=out_sb[:], in0=psP[:], scalar=fd_all[:, t:t + 1],
                in1=x2b[:], op0=ALU.mult, op1=ALU.add),
                reads=[("psP",), ("fd", t), ("x2b",)], writes=[("out_sb",)])
            if t == 8:
                D("out_sb8", out_sb[:], [("out_sb",)])
                D("fd8", fd_all[:], [("fd", tt) for tt in range(NTILE)])
            dma("scalar", out_d[s, 128 * i:128 * i + 128, :], out_sb[:],
                reads=[("out_sb",)], writes=[("out_d", t)])
        sc.close()

    p.barrier()
    p.build(ctx)
    ctx.close()
    return nc, dbg_outs


def make_inputs(full_inputs, core):
    f32 = lambda a: np.ascontiguousarray(np.asarray(a, dtype=np.float32))
    x = f32(full_inputs["L0_attn_x"])
    m = {
        "x": np.ascontiguousarray(x[SEQ * core:SEQ * (core + 1)]),
        "wq": f32(full_inputs["L0_attn_Wq"]),
        "wqr": f32(full_inputs["L0_attn_Wqr"]),
        "wkr": f32(full_inputs["L0_attn_Wkr"]),
        "wkvd": f32(full_inputs["L0_attn_Wkvd"]),
        "wku": f32(full_inputs["L0_attn_Wku"]),
        "wvu": f32(full_inputs["L0_attn_Wvu"]),
        "wo": f32(full_inputs["L0_attn_Wo"]),
        "lobo": f32(full_inputs["L0_attn_lobo_log"]).reshape(1, H),
        "wgT": np.ascontiguousarray(f32(full_inputs["L0_ffn_w_gate"]).T),
        "wuT": np.ascontiguousarray(f32(full_inputs["L0_ffn_w_up"]).T),
        "wdT": np.ascontiguousarray(f32(full_inputs["L0_ffn_w_down"]).T),
    }
    m.update(host_constants())
    return m


def kernel(**inputs):
    nc, _ = build_program()
    in_maps = [make_inputs(inputs, c) for c in range(NCORE)]
    res = run_bass_kernel_spmd(nc, in_maps, core_ids=list(range(NCORE)))
    out = np.concatenate([np.asarray(r["out"]) for r in res.results], axis=0)
    return out.astype(np.float32)
